# version 06 (final)
# baseline (speedup 1.0000x reference)
import numpy as np
import ml_dtypes
from contextlib import ExitStack
import concourse.bass as bass
import concourse.mybir as mybir
from concourse.bass_utils import run_bass_kernel_spmd

F32 = mybir.dt.float32
BF16 = mybir.dt.bfloat16
I32 = mybir.dt.int32
AF = mybir.ActivationFunctionType
ALU = mybir.AluOpType
AX = mybir.AxisListType
BF = ml_dtypes.bfloat16

T = 4096
D = 1024
KD = D // 128
NCORES = 8
ENGINES = ("sync", "tensor", "vector", "scalar", "gpsimd")


NDMAC = 24


class Em:
    def __init__(self, nc, stack):
        self.nc = nc
        self.stack = stack
        self.sems = {e: stack.enter_context(nc.semaphore("sem_" + e)) for e in ENGINES}
        self.dsems = [stack.enter_context(nc.semaphore(f"dsem{i}")) for i in range(NDMAC)]
        self.cnt = {e: 0 for e in ENGINES}
        self.dcnt = [0] * NDMAC
        self.dclass = {}
        self.next_class = 0
        self.waited = {}
        self.W = {}
        self.R = {}
        self.prog = {e: [] for e in ENGINES}

    def sb(self, name, p, f, dt):
        return self.stack.enter_context(self.nc.sbuf_tensor(name, [p, f], dt))

    def ps(self, name, p, f, dt=F32):
        return self.stack.enter_context(self.nc.psum_tensor(name, [p, f], dt))

    def _need(self, eng, need, t):
        if t is None:
            return
        kind, key, v = t
        if kind == "e":
            if key == eng == "tensor":
                return
            k = ("e", key)
            need[k] = max(need.get(k, 0), v)
        else:
            k = ("d", key)
            need[k] = max(need.get(k, 0), self.dcnt[key])

    def _waits_for(self, eng, rd, wr):
        need = {}
        for h in rd:
            self._need(eng, need, self.W.get(id(h)))
        for h in wr:
            self._need(eng, need, self.W.get(id(h)))
            for t in self.R.get(id(h), ()):
                self._need(eng, need, t)
        waits = []
        for (kind, key), v in need.items():
            wk = (eng, kind, key)
            if self.waited.get(wk, 0) < v:
                sem = self.sems[key] if kind == "e" else self.dsems[key]
                waits.append((sem, v))
                self.waited[wk] = v
        return waits

    def op(self, eng, opname, *args, rd=(), wr=(), kw=None):
        kw = kw or {}
        waits = self._waits_for(eng, rd, wr)
        self.cnt[eng] += 1
        sem = self.sems[eng]
        self.prog[eng].append((waits, opname, args, kw, sem, 1))
        tok = ("e", eng, self.cnt[eng])
        for h in rd:
            self.R.setdefault(id(h), []).append(tok)
        for h in wr:
            self.W[id(h)] = tok
            self.R[id(h)] = []
        return tok

    def dma(self, eng, dst, src, rd=(), wr=(), key=None, kw=None):
        kw = kw or {}
        waits = self._waits_for(eng, rd, wr)
        if key is None:
            key = wr[0] if wr else (rd[0] if rd else dst)
        ki = id(key)
        if ki not in self.dclass:
            self.dclass[ki] = self.next_class % NDMAC
            self.next_class += 1
        ci = self.dclass[ki]
        self.dcnt[ci] += 16
        self.prog[eng].append((waits, "dma_start", (dst, src), kw, self.dsems[ci], 16))
        tok = ("d", ci, self.dcnt[ci])
        for h in rd:
            self.R.setdefault(id(h), []).append(tok)
        for h in wr:
            self.W[id(h)] = tok
            self.R[id(h)] = []
        return tok

    def barrier(self):
        snap_e = dict(self.cnt)
        snap_d = list(self.dcnt)
        for e in ENGINES:
            waits = []
            for pe in ENGINES:
                if pe == e or snap_e[pe] == 0:
                    continue
                wk = (e, "e", pe)
                if self.waited.get(wk, 0) < snap_e[pe]:
                    waits.append((self.sems[pe], snap_e[pe]))
                    self.waited[wk] = snap_e[pe]
            for ci in range(NDMAC):
                if snap_d[ci] == 0:
                    continue
                wk = (e, "d", ci)
                if self.waited.get(wk, 0) < snap_d[ci]:
                    waits.append((self.dsems[ci], snap_d[ci]))
                    self.waited[wk] = snap_d[ci]
            if waits:
                self.prog[e].append((waits, None, None, None, None, 0))
        self.W = {}
        self.R = {}

    def finalize(self):
        waits = []
        for pe in ENGINES:
            if self.cnt[pe] == 0 or pe == "gpsimd":
                continue
            wk = ("gpsimd", "e", pe)
            if self.waited.get(wk, 0) < self.cnt[pe]:
                waits.append((self.sems[pe], self.cnt[pe]))
        for ci in range(NDMAC):
            if self.dcnt[ci] == 0:
                continue
            wk = ("gpsimd", "d", ci)
            if self.waited.get(wk, 0) < self.dcnt[ci]:
                waits.append((self.dsems[ci], self.dcnt[ci]))
        self.prog["gpsimd"].append((waits, "__waitself__", (), {}, None, 0))
        final_gp = self.cnt["gpsimd"]
        nc = self.nc
        with nc.Block() as block:
            def mk(engname):
                def run(e):
                    for waits, opname, args, kw, sem, inc in self.prog[engname]:
                        for s, v in waits:
                            e.wait_ge(s, v)
                        if opname is None:
                            continue
                        if opname == "__waitself__":
                            if final_gp > 0:
                                e.wait_ge(self.sems["gpsimd"], final_gp)
                            continue
                        ins = getattr(e, opname)(*args, **kw)
                        if sem is not None:
                            ins.then_inc(sem, inc)
                return run
            block.sync(mk("sync"))
            block.tensor(mk("tensor"))
            block.vector(mk("vector"))
            block.scalar(mk("scalar"))
            block.gpsimd(mk("gpsimd"))


def load_w_bf(em, name, dram, kblocks, ncols, stage_f32):
    tiles = []
    for kb in range(kblocks):
        t = em.sb(f"{name}_bf{kb}", 128, ncols, BF16)
        em.dma("sync", stage_f32[:, :ncols], dram[kb * 128:(kb + 1) * 128, :],
               rd=(stage_f32,), wr=(stage_f32,))
        em.op("vector", "tensor_copy", t[:, :], stage_f32[:, :ncols],
              rd=(stage_f32,), wr=(t,))
        tiles.append(t)
    return tiles


def emit_ln(em, x_tiles, u_tiles, yn_tiles, TC, ps_s1, ps_s2, ps_b1, ps_b2,
            sq_buf, rows, ones_col, ones_row, g_sb=None, gcol=0, b_sb=None,
            bcol=0, nblk=KD, Dn=D):
    for kb in range(nblk):
        em.op("tensor", "matmul", ps_s1[0:1, 0:TC], ones_col[:, 0:1],
              x_tiles[kb][:, 0:TC], rd=(ones_col, x_tiles[kb]), wr=(ps_s1,),
              kw=dict(start=(kb == 0), stop=(kb == nblk - 1)))
    for kb in range(nblk):
        em.op("scalar", "activation", sq_buf[kb % 2][:, 0:TC], x_tiles[kb][:, 0:TC],
              AF.Square, rd=(x_tiles[kb],), wr=(sq_buf[kb % 2],))
        em.op("tensor", "matmul", ps_s2[0:1, 0:TC], ones_col[:, 0:1],
              sq_buf[kb % 2][:, 0:TC], rd=(ones_col, sq_buf[kb % 2]), wr=(ps_s2,),
              kw=dict(start=(kb == 0), stop=(kb == nblk - 1)))
    mu, msq, var, rstd = rows
    em.op("scalar", "activation", mu[0:1, 0:TC], ps_s1[0:1, 0:TC], AF.Copy,
          rd=(ps_s1,), wr=(mu,), kw=dict(scale=1.0 / Dn))
    em.op("scalar", "activation", msq[0:1, 0:TC], mu[0:1, 0:TC], AF.Square,
          rd=(mu,), wr=(msq,))
    em.op("vector", "scalar_tensor_tensor", var[0:1, 0:TC], ps_s2[0:1, 0:TC],
          1.0 / Dn, msq[0:1, 0:TC], ALU.mult, ALU.subtract,
          rd=(ps_s2, msq), wr=(var,))
    em.op("vector", "tensor_scalar_add", var[0:1, 0:TC], var[0:1, 0:TC], 1e-5,
          rd=(var,), wr=(var,))
    em.op("scalar", "activation", var[0:1, 0:TC], var[0:1, 0:TC], AF.Sqrt,
          rd=(var,), wr=(var,))
    em.op("vector", "reciprocal", rstd[0:1, 0:TC], var[0:1, 0:TC],
          rd=(var,), wr=(rstd,))
    em.op("tensor", "matmul", ps_b1[:, 0:TC], ones_row[0:1, :], mu[0:1, 0:TC],
          rd=(ones_row, mu), wr=(ps_b1,), kw=dict(start=True, stop=True))
    em.op("tensor", "matmul", ps_b2[:, 0:TC], ones_row[0:1, :], rstd[0:1, 0:TC],
          rd=(ones_row, rstd), wr=(ps_b2,), kw=dict(start=True, stop=True))
    for kb in range(nblk):
        u = u_tiles[kb]
        em.op("vector", "tensor_tensor", u[:, 0:TC], x_tiles[kb][:, 0:TC],
              ps_b1[:, 0:TC], ALU.subtract, rd=(x_tiles[kb], ps_b1), wr=(u,))
        em.op("vector", "tensor_tensor", u[:, 0:TC], u[:, 0:TC], ps_b2[:, 0:TC],
              ALU.mult, rd=(u, ps_b2), wr=(u,))
        if yn_tiles is not None:
            yn = yn_tiles[kb]
            if g_sb is not None:
                em.op("vector", "tensor_scalar", yn[:, 0:TC], u[:, 0:TC],
                      g_sb[:, gcol + kb:gcol + kb + 1],
                      b_sb[:, bcol + kb:bcol + kb + 1], ALU.mult, ALU.add,
                      rd=(u, g_sb, b_sb), wr=(yn,))
            else:
                em.op("vector", "tensor_copy", yn[:, 0:TC], u[:, 0:TC],
                      rd=(u,), wr=(yn,))


def common_consts(em):
    ones_col = em.sb("ones_col", 128, 1, F32)
    ones_row = em.sb("ones_row", 1, 128, F32)
    em.op("vector", "memset", ones_col[:, :], 1.0, wr=(ones_col,))
    em.op("vector", "memset", ones_row[0:1, :], 1.0, wr=(ones_row,))
    return ones_col, ones_row


def ln_scratch(em, TC, tag=""):
    ps_s1 = em.ps("ps_s1" + tag, 1, 512)
    ps_s2 = em.ps("ps_s2" + tag, 1, 512)
    ps_b1 = em.ps("ps_b1" + tag, 128, 512)
    ps_b2 = em.ps("ps_b2" + tag, 128, 512)
    sq = [em.sb(f"lnsq{i}" + tag, 128, TC, F32) for i in range(2)]
    rows = [em.sb(f"lnrow{i}" + tag, 1, TC, F32) for i in range(4)]
    return ps_s1, ps_s2, ps_b1, ps_b2, sq, rows


def build_phaseA(T=T):
    nc = bass.Bass("TRN2", target_bir_lowering=False)
    TC = 256
    NT = T // TC
    xT = nc.dram_tensor("xT", [D, T], F32, kind="ExternalInput")
    wn = {}
    for w in ("Wq", "Wk", "Wv", "Wg", "Wo"):
        wn[w] = nc.dram_tensor(w, [D, D], F32, kind="ExternalInput")
    for b in ("bq", "bk", "bv", "bg", "bo"):
        wn[b] = nc.dram_tensor(b, [128, KD], F32, kind="ExternalInput")
    lng = nc.dram_tensor("ln1_g", [128, KD], F32, kind="ExternalInput")
    lnb = nc.dram_tensor("ln1_b", [128, KD], F32, kind="ExternalInput")
    dec = nc.dram_tensor("decay_rep", [128, KD], F32, kind="ExternalInput")
    x2T = nc.dram_tensor("x2T", [D, T], F32, kind="ExternalOutput")
    with ExitStack() as stack:
        em = Em(nc, stack)
        ones_col, ones_row = common_consts(em)
        stage = em.sb("stage", 128, D, F32)
        Wsb = {}
        for w in ("Wq", "Wk", "Wv", "Wg", "Wo"):
            Wsb[w] = load_w_bf(em, w, wn[w], KD, D, stage)
        small = em.sb("small", 128, 8 * KD, F32)
        names = ["bq", "bk", "bv", "bg", "bo"]
        for i, nm in enumerate(names):
            em.dma("sync", small[:, i * KD:(i + 1) * KD], wn[nm][:, :], wr=(small,))
        em.dma("sync", small[:, 5 * KD:6 * KD], lng[:, :], wr=(small,))
        em.dma("sync", small[:, 6 * KD:7 * KD], lnb[:, :], wr=(small,))
        em.dma("sync", small[:, 7 * KD:8 * KD], dec[:, :], wr=(small,))
        em.op("scalar", "activation", small[:, 7 * KD:8 * KD],
              small[:, 7 * KD:8 * KD], AF.Sigmoid, rd=(small,), wr=(small,))
        COL = {nm: i * KD for i, nm in enumerate(names + ["lng", "lnb", "dec"])}
        onesT = em.sb("onesT", 128, TC, F32)
        em.op("vector", "memset", onesT[:, :], 1.0, wr=(onesT,))
        dtile = [em.sb(f"dtile{kb}", 128, TC, F32) for kb in range(KD)]
        for kb in range(KD):
            em.op("vector", "tensor_scalar", dtile[kb][:, :], onesT[:, :],
                  small[:, 7 * KD + kb:7 * KD + kb + 1], None, ALU.mult,
                  rd=(onesT, small), wr=(dtile[kb],))
        state = em.sb("state", 128, KD, F32)
        em.op("vector", "memset", state[:, :], 0.0, wr=(state,))
        xt = [em.sb(f"xt{k}", 128, TC, F32) for k in range(KD)]
        ut = [em.sb(f"ut{k}", 128, TC, F32) for k in range(KD)]
        yn = [em.sb(f"yn{k}", 128, TC, BF16) for k in range(KD)]
        qt = [em.sb(f"qt{k}", 128, TC, BF16) for k in range(KD)]
        kt = [em.sb(f"kt{k}", 128, TC, BF16) for k in range(KD)]
        vt = [em.sb(f"vt{k}", 128, TC, BF16) for k in range(KD)]
        gt = [em.sb(f"gt{k}", 128, TC, BF16) for k in range(KD)]
        kv = [em.sb(f"kv{k}", 128, TC, F32) for k in range(KD)]
        sc = [em.sb(f"sc{k}", 128, TC, F32) for k in range(KD)]
        gq = [em.sb(f"gq{k}", 128, TC, F32) for k in range(KD)]
        ob = [em.sb(f"ob{k}", 128, TC, BF16) for k in range(KD)]
        o2 = [em.sb(f"o2{k}", 128, TC, F32) for k in range(2)]
        lns = ln_scratch(em, TC)
        ps_mm = em.ps("ps_mm", 128, 512)
        ps_mm2 = em.ps("ps_mm2", 128, 512)
        for c in range(NT):
            t0 = c * TC
            for kb in range(KD):
                em.dma("sync", xt[kb][:, :], xT[kb * 128:(kb + 1) * 128, t0:t0 + TC],
                       rd=(xt[kb],), wr=(xt[kb],))
            emit_ln(em, xt, ut, yn, TC, *lns, ones_col, ones_row,
                    g_sb=small, gcol=5 * KD, b_sb=small, bcol=6 * KD)
            for (W, bcol, dst, act) in (
                ("Wq", COL["bq"], qt, AF.Copy),
                ("Wk", COL["bk"], kt, AF.Copy),
                ("Wv", COL["bv"], vt, AF.Copy),
                ("Wg", COL["bg"], gt, AF.Sigmoid),
            ):
                for mb in range(KD):
                    pp = ps_mm if (mb % 2 == 0) else ps_mm2
                    for kb in range(KD):
                        em.op("tensor", "matmul", pp[:, 0:TC],
                              Wsb[W][kb][:, mb * 128:(mb + 1) * 128],
                              yn[kb][:, 0:TC], rd=(Wsb[W][kb], yn[kb]), wr=(pp,),
                              kw=dict(start=(kb == 0), stop=(kb == KD - 1)))
                    if act == AF.Copy:
                        em.op("vector", "tensor_scalar", dst[mb][:, 0:TC],
                              pp[:, 0:TC], small[:, bcol + mb:bcol + mb + 1],
                              None, ALU.add, rd=(pp, small), wr=(dst[mb],))
                    else:
                        em.op("scalar", "activation", dst[mb][:, 0:TC], pp[:, 0:TC],
                              act, rd=(pp, small), wr=(dst[mb],),
                              kw=dict(bias=small[:, bcol + mb:bcol + mb + 1]))
            for kb in range(KD):
                em.op("vector", "tensor_tensor", kv[kb][:, :], kt[kb][:, :],
                      vt[kb][:, :], ALU.mult, rd=(kt[kb], vt[kb]), wr=(kv[kb],))
                em.op("vector", "scalar_tensor_tensor", gq[kb][:, :], gt[kb][:, :],
                      0.125, qt[kb][:, :], ALU.mult, ALU.mult,
                      rd=(gt[kb], qt[kb]), wr=(gq[kb],))
                em.op("vector", "tensor_tensor_scan", sc[kb][:, :], dtile[kb][:, :],
                      kv[kb][:, :], state[:, kb:kb + 1], ALU.mult, ALU.add,
                      rd=(dtile[kb], kv[kb], state), wr=(sc[kb],))
                em.op("vector", "tensor_copy", state[:, kb:kb + 1],
                      sc[kb][:, TC - 1:TC], rd=(sc[kb],), wr=(state,))
                em.op("vector", "tensor_tensor", ob[kb][:, :], gq[kb][:, :],
                      sc[kb][:, :], ALU.mult, rd=(gq[kb], sc[kb]), wr=(ob[kb],))
            for mb in range(KD):
                pp = ps_mm if (mb % 2 == 0) else ps_mm2
                for kb in range(KD):
                    em.op("tensor", "matmul", pp[:, 0:TC],
                          Wsb["Wo"][kb][:, mb * 128:(mb + 1) * 128],
                          ob[kb][:, 0:TC], rd=(Wsb["Wo"][kb], ob[kb]), wr=(pp,),
                          kw=dict(start=(kb == 0), stop=(kb == KD - 1)))
                dst = o2[mb % 2]
                em.op("vector", "tensor_scalar", dst[:, 0:TC], pp[:, 0:TC],
                      small[:, COL["bo"] + mb:COL["bo"] + mb + 1], None,
                      ALU.add, rd=(pp, small), wr=(dst,))
                em.op("vector", "tensor_tensor", dst[:, 0:TC], dst[:, 0:TC],
                      ut[mb][:, 0:TC], ALU.add, rd=(dst, ut[mb]), wr=(dst,))
                em.dma("gpsimd", x2T[mb * 128:(mb + 1) * 128, t0:t0 + TC],
                       dst[:, 0:TC], rd=(dst,))
        em.finalize()
    return nc


def build_ffn(quant, tail, T=T):
    nc = bass.Bass("TRN2", target_bir_lowering=False)
    TC = 512
    NT = T // TC
    FF = 4096
    KF = FF // 128
    xT = nc.dram_tensor("xT", [D, T], F32, kind="ExternalInput")
    if tail:
        x2T = nc.dram_tensor("x2T", [D, T], F32, kind="ExternalInput")
    g_d = nc.dram_tensor("g", [128, KD], F32, kind="ExternalInput")
    b_d = nc.dram_tensor("b", [128, KD], F32, kind="ExternalInput")
    b1_d = nc.dram_tensor("b1", [128, KF], F32, kind="ExternalInput")
    b2_d = nc.dram_tensor("b2", [128, KD], F32, kind="ExternalInput")
    if quant:
        q1_d = nc.dram_tensor("fc1_q", [D, FF], I32, kind="ExternalInput")
        s1_d = nc.dram_tensor("fc1_scales", [KD, FF], F32, kind="ExternalInput")
        q2_d = nc.dram_tensor("fc2_q", [FF, D], I32, kind="ExternalInput")
        s2_d = nc.dram_tensor("fc2_scales", [32, D], F32, kind="ExternalInput")
    else:
        W1_d = nc.dram_tensor("W1", [D, FF], F32, kind="ExternalInput")
        W2_d = nc.dram_tensor("W2", [FF, D], F32, kind="ExternalInput")
    out_d = nc.dram_tensor("outT", [D, T], F32, kind="ExternalOutput")
    w2bf = nc.dram_tensor("w2bf", [FF, D], BF16)
    act = AF.Silu if quant else AF.Gelu
    s_out = 0.5 if quant else 1.0
    with ExitStack() as stack:
        em = Em(nc, stack)
        ones_col, ones_row = common_consts(em)
        W1sb = [em.sb(f"W1sb{kb}", 128, FF, BF16) for kb in range(KD)]
        ps_h = em.ps("ps_h", 128, 512)
        ps_h2 = em.ps("ps_h2", 128, 512)
        ps_z = [em.ps(f"ps_z{i}", 128, 512) for i in range(2)]
        ps_bc = ps_h
        if quant:
            qbuf = em.sb("qbuf", 128, FF, I32)
            fbuf = em.sb("fbuf", 128, FF, F32)
            srow = em.sb("srow", 1, FF, F32)
            for kb in range(KD):
                em.dma("sync", qbuf[:, :], q1_d[kb * 128:(kb + 1) * 128, :],
                       rd=(qbuf,), wr=(qbuf,))
                em.dma("sync", srow[0:1, :], s1_d[kb:kb + 1, :], rd=(srow,), wr=(srow,))
                em.op("vector", "tensor_copy", fbuf[:, :], qbuf[:, :],
                      rd=(qbuf,), wr=(fbuf,))
                for h in range(KF):
                    em.op("tensor", "matmul", ps_bc[:, 0:128], ones_row[0:1, :],
                          srow[0:1, h * 128:(h + 1) * 128], rd=(ones_row, srow),
                          wr=(ps_bc,), kw=dict(start=True, stop=True))
                    em.op("vector", "scalar_tensor_tensor",
                          W1sb[kb][:, h * 128:(h + 1) * 128],
                          fbuf[:, h * 128:(h + 1) * 128], -128.0, ps_bc[:, 0:128],
                          ALU.add, ALU.mult, rd=(fbuf, ps_bc), wr=(W1sb[kb],))
            s2row = em.sb("s2row", 1, D, F32)
            w2stage = em.sb("w2stage", 128, D, BF16)
            for kb in range(32):
                em.dma("sync", qbuf[:, 0:D], q2_d[kb * 128:(kb + 1) * 128, :],
                       rd=(qbuf,), wr=(qbuf,))
                em.dma("sync", s2row[0:1, :], s2_d[kb:kb + 1, :], rd=(s2row,), wr=(s2row,))
                em.op("vector", "tensor_copy", fbuf[:, 0:D], qbuf[:, 0:D],
                      rd=(qbuf,), wr=(fbuf,))
                for h in range(2):
                    em.op("tensor", "matmul", ps_bc[:, 0:512], ones_row[0:1, :],
                          s2row[0:1, h * 512:(h + 1) * 512], rd=(ones_row, s2row),
                          wr=(ps_bc,), kw=dict(start=True, stop=True))
                    em.op("vector", "scalar_tensor_tensor",
                          w2stage[:, h * 512:(h + 1) * 512],
                          fbuf[:, h * 512:(h + 1) * 512], -128.0, ps_bc[:, 0:512],
                          ALU.add, ALU.mult, rd=(fbuf, ps_bc), wr=(w2stage,))
                em.dma("gpsimd", w2bf[kb * 128:(kb + 1) * 128, :], w2stage[:, :],
                       rd=(w2stage,), wr=(w2bf,))
        else:
            fbuf = em.sb("fbuf", 128, FF, F32)
            for kb in range(KD):
                em.dma("sync", fbuf[:, :], W1_d[kb * 128:(kb + 1) * 128, :],
                       rd=(fbuf,), wr=(fbuf,))
                em.op("vector", "tensor_copy", W1sb[kb][:, :], fbuf[:, :],
                      rd=(fbuf,), wr=(W1sb[kb],))
            w2stage = em.sb("w2stage", 128, D, BF16)
            for kb in range(32):
                em.dma("sync", fbuf[:, 0:D], W2_d[kb * 128:(kb + 1) * 128, :],
                       rd=(fbuf,), wr=(fbuf,))
                em.op("vector", "tensor_copy", w2stage[:, :], fbuf[:, 0:D],
                      rd=(fbuf,), wr=(w2stage,))
                em.dma("gpsimd", w2bf[kb * 128:(kb + 1) * 128, :], w2stage[:, :],
                       rd=(w2stage,), wr=(w2bf,))
        em.barrier()
        small = em.sb("small", 128, 2 * KD + KF + KD, F32)
        em.dma("sync", small[:, 0:KD], g_d[:, :], wr=(small,))
        em.dma("sync", small[:, KD:2 * KD], b_d[:, :], wr=(small,))
        em.dma("sync", small[:, 2 * KD:2 * KD + KF], b1_d[:, :], wr=(small,))
        em.dma("sync", small[:, 2 * KD + KF:], b2_d[:, :], wr=(small,))
        C_G, C_B, C_B1, C_B2 = 0, KD, 2 * KD, 2 * KD + KF
        xt = [em.sb(f"xt{k}", 128, TC, F32) for k in range(KD)]
        if tail:
            x2t = [em.sb(f"x2t{k}", 128, TC, F32) for k in range(KD)]
        ut = [fbuf[:, k * TC:(k + 1) * TC] for k in range(KD)]
        yn = [em.sb(f"yn{k}", 128, TC, BF16) for k in range(KD)]
        ht = [em.sb(f"ht{k}", 128, TC, BF16) for k in range(KF)]
        w2t = [em.sb(f"w2t{i}", 128, D, BF16) for i in range(3)]
        o1 = [em.sb(f"o1{k}", 128, TC, F32) for k in range(2)]
        lns = ln_scratch(em, TC)
        for c in range(NT):
            t0 = c * TC
            for kb in range(KD):
                em.dma("sync", xt[kb][:, :], xT[kb * 128:(kb + 1) * 128, t0:t0 + TC],
                       rd=(xt[kb],), wr=(xt[kb],))
                if tail:
                    em.dma("sync", x2t[kb][:, :],
                           x2T[kb * 128:(kb + 1) * 128, t0:t0 + TC],
                           rd=(x2t[kb],), wr=(x2t[kb],))
            src = x2t if tail else xt
            emit_ln(em, src, ut, yn, TC, *lns, ones_col, ones_row,
                    g_sb=small, gcol=C_G, b_sb=small, bcol=C_B)
            for mb in range(KF):
                pp = ps_h if (mb % 2 == 0) else ps_h2
                for kb in range(KD):
                    em.op("tensor", "matmul", pp[:, 0:TC],
                          W1sb[kb][:, mb * 128:(mb + 1) * 128], yn[kb][:, 0:TC],
                          rd=(W1sb[kb], yn[kb]), wr=(pp,),
                          kw=dict(start=(kb == 0), stop=(kb == KD - 1)))
                em.op("scalar", "activation", ht[mb][:, 0:TC], pp[:, 0:TC], act,
                      rd=(pp, small), wr=(ht[mb],),
                      kw=dict(bias=small[:, C_B1 + mb:C_B1 + mb + 1]))
            for db in range(KD):
                pz = ps_z[db % 2]
                for hb in range(KF):
                    wt = w2t[hb % 3]
                    em.dma("sync", wt[:, :], w2bf[hb * 128:(hb + 1) * 128, :],
                           rd=(wt, w2bf), wr=(wt,))
                    em.op("tensor", "matmul", pz[:, 0:TC],
                          wt[:, db * 128:(db + 1) * 128], ht[hb][:, 0:TC],
                          rd=(wt, ht[hb]), wr=(pz,),
                          kw=dict(start=(hb == 0), stop=(hb == KF - 1)))
                dst = o1[db % 2]
                em.op("vector", "tensor_scalar", dst[:, 0:TC], pz[:, 0:TC],
                      small[:, C_B2 + db:C_B2 + db + 1], s_out, ALU.add, ALU.mult,
                      rd=(pz, small), wr=(dst,))
                res1 = x2t[db] if tail else ut[db]
                em.op("vector", "tensor_tensor", dst[:, 0:TC], dst[:, 0:TC],
                      res1[:, 0:TC], ALU.add, rd=(dst, res1), wr=(dst,))
                em.op("vector", "tensor_tensor", dst[:, 0:TC], dst[:, 0:TC],
                      xt[db][:, 0:TC], ALU.add, rd=(dst, xt[db]), wr=(dst,))
                em.dma("gpsimd", out_d[db * 128:(db + 1) * 128, t0:t0 + TC],
                       dst[:, 0:TC], rd=(dst,))
        em.finalize()
    return nc


def build_sink(T=T, NIT=20):
    nc = bass.Bass("TRN2", target_bir_lowering=False)
    NH = 8
    IB = T // 128
    JC = T // 512
    TC = 512
    NT = T // TC
    xT = nc.dram_tensor("xT", [D, T], F32, kind="ExternalInput")
    Wq_d = nc.dram_tensor("Wq", [D, 512], F32, kind="ExternalInput")
    Wk_d = nc.dram_tensor("Wk", [D, 512], F32, kind="ExternalInput")
    Wv_d = nc.dram_tensor("Wv", [D, 512], F32, kind="ExternalInput")
    bq_d = nc.dram_tensor("bq", [64, NH], F32, kind="ExternalInput")
    bk_d = nc.dram_tensor("bk", [64, NH], F32, kind="ExternalInput")
    bv_d = nc.dram_tensor("bv", [1, 512], F32, kind="ExternalInput")
    Wo_d = nc.dram_tensor("Wo", [512, D], F32, kind="ExternalInput")
    bo_d = nc.dram_tensor("bo", [128, KD], F32, kind="ExternalInput")
    identb_d = nc.dram_tensor("identb", [128, 128], BF16, kind="ExternalInput")
    identf_d = nc.dram_tensor("identf", [128, 128], F32, kind="ExternalInput")
    out_d = nc.dram_tensor("outT", [D, T], F32, kind="ExternalOutput")
    EId = nc.dram_tensor("EId", [NH * T, T], BF16)
    EJd = nc.dram_tensor("EJd", [NH * T, T], BF16)
    vtokd = nc.dram_tensor("vtokd", [T, 512], BF16)
    qhd = nc.dram_tensor("qhd", [NH * 64, T], BF16)
    khd = nc.dram_tensor("khd", [NH * 64, T], BF16)
    aoTd = nc.dram_tensor("aoTd", [NH * 64, T], BF16)
    wtmp = nc.dram_tensor("wtmp", [IB, 128], F32)
    with ExitStack() as stack:
        em = Em(nc, stack)
        ones_col, ones_row = common_consts(em)
        identb = em.sb("identb_s", 128, 128, BF16)
        identf = em.sb("identf_s", 128, 128, F32)
        em.dma("sync", identb[:, :], identb_d[:, :], wr=(identb,))
        em.dma("sync", identf[:, :], identf_d[:, :], wr=(identf,))
        stageW = em.sb("stageW", 128, 512, F32)
        Wqsb = load_w_bf(em, "Wqs", Wq_d, KD, 512, stageW)
        Wksb = load_w_bf(em, "Wks", Wk_d, KD, 512, stageW)
        Wvsb = load_w_bf(em, "Wvs", Wv_d, KD, 512, stageW)
        bsm = em.sb("bsm", 64, 2 * NH, F32)
        em.dma("sync", bsm[:, 0:NH], bq_d[:, :], wr=(bsm,))
        em.dma("sync", bsm[:, NH:2 * NH], bk_d[:, :], wr=(bsm,))
        bvrow = em.sb("bvrow", 1, 512, F32)
        em.dma("sync", bvrow[0:1, :], bv_d[:, :], wr=(bvrow,))
        bvb = em.sb("bvb", 128, 512, F32)
        ps_a = em.ps("ps_a", 128, 512)
        ps_b = em.ps("ps_b", 128, 512)
        ps_s1 = em.ps("ps_s1", 1, 512)
        ps_s2 = em.ps("ps_s2", 1, 512)
        ps_tr = em.ps("ps_tr", 64, 128, BF16)
        ps_ao = em.ps("ps_ao", 128, 64)
        sqb = [em.sb(f"lnsq{i}", 128, TC, F32) for i in range(2)]
        rows = [em.sb(f"lnrow{i}", 1, TC, F32) for i in range(4)]
        lns = (ps_s1, ps_s2, ps_a, ps_b, sqb, rows)
        em.op("tensor", "matmul", ps_a[:, :], ones_row[0:1, :], bvrow[0:1, :],
              rd=(ones_row, bvrow), wr=(ps_a,), kw=dict(start=True, stop=True))
        em.op("scalar", "copy", bvb[:, :], ps_a[:, :], rd=(ps_a,), wr=(bvb,))
        xt = [em.sb(f"xt{k}", 128, TC, F32) for k in range(KD)]
        ut = [em.sb(f"ut{k}", 128, TC, F32) for k in range(KD)]
        yn = [em.sb(f"yn{k}", 128, TC, BF16) for k in range(KD)]
        vstage = em.sb("vstage", 128, 512, BF16)
        qkst = em.sb("qkst", 64, 512, BF16)
        for c in range(NT):
            t0 = c * TC
            for kb in range(KD):
                em.dma("sync", xt[kb][:, :], xT[kb * 128:(kb + 1) * 128, t0:t0 + TC],
                       rd=(xt[kb],), wr=(xt[kb],))
            emit_ln(em, xt, ut, yn, TC, *lns, ones_col, ones_row)
            for h in range(NH):
                for (Wsb, bcol, dstd) in ((Wqsb, 0, qhd), (Wksb, NH, khd)):
                    for kb in range(KD):
                        em.op("tensor", "matmul", ps_a[0:64, 0:TC],
                              Wsb[kb][:, h * 64:(h + 1) * 64], yn[kb][:, 0:TC],
                              rd=(Wsb[kb], yn[kb]), wr=(ps_a,),
                              kw=dict(start=(kb == 0), stop=(kb == KD - 1)))
                    em.op("vector", "tensor_scalar", qkst[:, 0:TC], ps_a[0:64, 0:TC],
                          bsm[:, bcol + h:bcol + h + 1], None, ALU.add,
                          rd=(ps_a, bsm), wr=(qkst,))
                    em.dma("gpsimd", dstd[h * 64:(h + 1) * 64, t0:t0 + TC],
                           qkst[:, 0:TC], rd=(qkst,), wr=(dstd,))
            for js in range(TC // 128):
                for kb in range(KD):
                    em.op("tensor", "matmul", ps_b[:, 0:512],
                          yn[kb][:, js * 128:(js + 1) * 128], Wvsb[kb][:, 0:512],
                          rd=(yn[kb], Wvsb[kb]), wr=(ps_b,),
                          kw=dict(start=(kb == 0), stop=(kb == KD - 1)))
                em.op("vector", "tensor_tensor", vstage[:, :], ps_b[:, :],
                      bvb[:, :], ALU.add, rd=(ps_b, bvb), wr=(vstage,))
                em.dma("gpsimd", vtokd[t0 + js * 128:t0 + (js + 1) * 128, :],
                       vstage[:, :], rd=(vstage,), wr=(vtokd,))
        em.barrier()
        ebuf = [em.sb(f"ebuf{i}", 128, 512, BF16) for i in range(3)]
        qtb = [em.sb(f"qtb{i}", 64, 128, BF16) for i in range(2)]
        ktb = [em.sb(f"ktb{i}", 64, 512, BF16) for i in range(2)]
        acc_cols = em.sb("acc_cols", 128, JC, F32)
        rs_all = em.sb("rs_all", 128, IB * NH, F32)
        wu_all = em.sb("wu_all", 128, IB * NH, F32)
        wv_all = em.sb("wv_all", 128, IB * NH, F32)
        wrow = em.sb("wrow", 1, T, F32)
        wub = em.sb("wub", 128, T, BF16)
        wvb = em.sb("wvb", 128, T, BF16)
        scrap = em.sb("scrap", 128, T, BF16)
        ejt = [em.sb(f"ejt{i}", 128, T, BF16) for i in range(3)]
        cc = em.sb("cc", 128, IB, F32)
        rr = em.sb("rr", 128, IB, F32)
        trow = em.sb("trow", 32, 128, F32)

        def bounce(em, src_cols, src_h, dstb):
            em.op("tensor", "transpose", ps_b[0:IB, 0:128], src_cols, identf[:, :],
                  rd=(src_h, identf), wr=(ps_b,))
            em.op("scalar", "copy", trow[0:IB, 0:128], ps_b[0:IB, 0:128],
                  rd=(ps_b,), wr=(trow,))
            em.dma("gpsimd", wtmp[:, :], trow[0:IB, 0:128], rd=(trow,), wr=(wtmp,))
            em.dma("sync", wrow[0:1, :], bass.AP(wtmp, 0, [[0, 1], [1, T]]),
                   rd=(wtmp,), wr=(wrow,))
            for ch in range(JC):
                em.op("tensor", "matmul", ps_a[:, 0:512], ones_row[0:1, :],
                      wrow[0:1, ch * 512:(ch + 1) * 512],
                      rd=(ones_row, wrow), wr=(ps_a,), kw=dict(start=True, stop=True))
                em.op("scalar", "copy", dstb[:, ch * 512:(ch + 1) * 512],
                      ps_a[:, 0:512], rd=(ps_a,), wr=(dstb,))

        for h in range(NH):
            for ib in range(IB):
                em.dma("sync", qtb[ib % 2][:, :],
                       qhd[h * 64:(h + 1) * 64, ib * 128:(ib + 1) * 128],
                       rd=(qtb[ib % 2], qhd), wr=(qtb[ib % 2],))
                for jc in range(JC):
                    kt_ = ktb[jc % 2]
                    em.dma("sync", kt_[:, :],
                           khd[h * 64:(h + 1) * 64, jc * 512:(jc + 1) * 512],
                           rd=(kt_, khd), wr=(kt_,))
                    eb = ebuf[jc % 3]
                    em.op("tensor", "matmul", ps_a[:, 0:512], qtb[ib % 2][:, :],
                          kt_[:, :], rd=(qtb[ib % 2], kt_), wr=(ps_a,),
                          kw=dict(start=True, stop=True))
                    em.op("scalar", "activation", eb[:, :], ps_a[:, 0:512], AF.Exp,
                          rd=(ps_a,), wr=(eb, acc_cols),
                          kw=dict(scale=0.125, accum_out=acc_cols[:, jc:jc + 1]))
                    em.dma("gpsimd",
                           EId[h * T + ib * 128:h * T + (ib + 1) * 128,
                               jc * 512:(jc + 1) * 512], eb[:, :],
                           rd=(eb,), wr=(EId,))
                em.op("vector", "reduce_sum", rs_all[:, h * IB + ib:h * IB + ib + 1],
                      acc_cols[:, 0:JC], rd=(acc_cols,), wr=(rs_all,),
                      kw=dict(axis=AX.X))
                em.dma("sync", qtb[ib % 2][:, :],
                       khd[h * 64:(h + 1) * 64, ib * 128:(ib + 1) * 128],
                       rd=(qtb[ib % 2], khd), wr=(qtb[ib % 2],))
                for jc in range(JC):
                    kt_ = ktb[jc % 2]
                    em.dma("sync", kt_[:, :],
                           qhd[h * 64:(h + 1) * 64, jc * 512:(jc + 1) * 512],
                           rd=(kt_, qhd), wr=(kt_,))
                    eb = ebuf[jc % 3]
                    em.op("tensor", "matmul", ps_b[:, 0:512], qtb[ib % 2][:, :],
                          kt_[:, :], rd=(qtb[ib % 2], kt_), wr=(ps_b,),
                          kw=dict(start=True, stop=True))
                    em.op("scalar", "activation", eb[:, :], ps_b[:, 0:512], AF.Exp,
                          rd=(ps_b,), wr=(eb,), kw=dict(scale=0.125))
                    em.dma("gpsimd",
                           EJd[h * T + ib * 128:h * T + (ib + 1) * 128,
                               jc * 512:(jc + 1) * 512], eb[:, :],
                           rd=(eb,), wr=(EJd,))
            hw = slice(h * IB, (h + 1) * IB)
            em.op("vector", "reciprocal", wu_all[:, hw], rs_all[:, hw],
                  rd=(rs_all,), wr=(wu_all,))
            bounce(em, wu_all[:, hw], wu_all, wub)
            for it in range(NIT):
                for jb in range(IB):
                    et = ejt[jb % 3]
                    em.dma("sync", et[:, :],
                           EJd[h * T + jb * 128:h * T + (jb + 1) * 128, :],
                           rd=(et, EJd), wr=(et,))
                    em.op("vector", "scalar_tensor_tensor", scrap[:, :], et[:, :],
                          1.0, wub[:, :], ALU.mult, ALU.mult,
                          rd=(et, wub), wr=(scrap, cc),
                          kw=dict(accum_out=cc[:, jb:jb + 1]))
                em.op("vector", "reciprocal", wv_all[:, hw], cc[:, 0:IB],
                      rd=(cc,), wr=(wv_all,))
                bounce(em, wv_all[:, hw], wv_all, wvb)
                for ib in range(IB):
                    et = ejt[ib % 3]
                    em.dma("sync", et[:, :],
                           EId[h * T + ib * 128:h * T + (ib + 1) * 128, :],
                           rd=(et, EId), wr=(et,))
                    em.op("vector", "scalar_tensor_tensor", scrap[:, :], et[:, :],
                          1.0, wvb[:, :], ALU.mult, ALU.mult,
                          rd=(et, wvb), wr=(scrap, rr),
                          kw=dict(accum_out=rr[:, ib:ib + 1]))
                em.op("vector", "reciprocal", wu_all[:, hw], rr[:, 0:IB],
                      rd=(rr,), wr=(wu_all,))
                if it < NIT - 1:
                    bounce(em, wu_all[:, hw], wu_all, wub)
            vw = scrap
            vt = ebuf[0]
            for jb in range(IB):
                em.dma("sync", vt[:, 0:64],
                       vtokd[jb * 128:(jb + 1) * 128, h * 64:(h + 1) * 64],
                       rd=(vt, vtokd), wr=(vt,))
                em.op("vector", "tensor_scalar", vw[:, jb * 64:(jb + 1) * 64],
                      vt[:, 0:64], wv_all[:, h * IB + jb:h * IB + jb + 1], None,
                      ALU.mult, rd=(vt, wv_all), wr=(vw,))
            ao_st = ebuf[1]
            for ib in range(IB):
                for jb in range(IB):
                    et = ejt[jb % 3]
                    em.dma("sync", et[:, 0:128],
                           EJd[h * T + jb * 128:h * T + (jb + 1) * 128,
                               ib * 128:(ib + 1) * 128],
                           rd=(et, EJd), wr=(et,))
                    em.op("tensor", "matmul", ps_ao[:, 0:64], et[:, 0:128],
                          vw[:, jb * 64:(jb + 1) * 64],
                          rd=(et, vw), wr=(ps_ao,),
                          kw=dict(start=(jb == 0), stop=(jb == IB - 1)))
                em.op("vector", "tensor_scalar", ao_st[:, 0:64], ps_ao[:, 0:64],
                      wu_all[:, h * IB + ib:h * IB + ib + 1], None, ALU.mult,
                      rd=(ps_ao, wu_all), wr=(ao_st,))
                em.op("tensor", "transpose", ps_tr[:, :], ao_st[:, 0:64],
                      identb[:, :], rd=(ao_st, identb), wr=(ps_tr,))
                em.op("scalar", "copy", qkst[:, 0:128], ps_tr[0:64, 0:128],
                      rd=(ps_tr,), wr=(qkst,))
                em.dma("gpsimd",
                       aoTd[h * 64:(h + 1) * 64, ib * 128:(ib + 1) * 128],
                       qkst[0:64, 0:128], rd=(qkst,), wr=(aoTd,))
        em.barrier()
        WoSb = []
        stageWo = em.sb("stageWo", 64, D, F32)
        for h in range(NH):
            t_ = em.sb(f"WoSb{h}", 64, D, BF16)
            em.dma("sync", stageWo[:, :], Wo_d[h * 64:(h + 1) * 64, :],
                   rd=(stageWo,), wr=(stageWo,))
            em.op("vector", "tensor_copy", t_[:, :], stageWo[:, :],
                  rd=(stageWo,), wr=(t_,))
            WoSb.append(t_)
        bosm = em.sb("bosm", 128, KD, F32)
        em.dma("sync", bosm[:, :], bo_d[:, :], wr=(bosm,))
        aotb = [em.sb(f"aotb{i}", 64, 512, BF16) for i in range(NH)]
        o1 = [em.sb(f"o1s{k}", 128, TC, F32) for k in range(2)]
        for c in range(NT):
            t0 = c * TC
            for kb in range(KD):
                em.dma("sync", xt[kb][:, :], xT[kb * 128:(kb + 1) * 128, t0:t0 + TC],
                       rd=(xt[kb],), wr=(xt[kb],))
            emit_ln(em, xt, ut, None, TC, *lns, ones_col, ones_row)
            for h in range(NH):
                em.dma("sync", aotb[h][:, :],
                       aoTd[h * 64:(h + 1) * 64, t0:t0 + TC],
                       rd=(aotb[h], aoTd), wr=(aotb[h],))
            for db in range(KD):
                pp = ps_a if db % 2 == 0 else ps_b
                for h in range(NH):
                    em.op("tensor", "matmul", pp[:, 0:TC],
                          WoSb[h][:, db * 128:(db + 1) * 128], aotb[h][:, :],
                          rd=(WoSb[h], aotb[h]), wr=(pp,),
                          kw=dict(start=(h == 0), stop=(h == NH - 1)))
                dst = o1[db % 2]
                em.op("vector", "tensor_scalar", dst[:, :], pp[:, 0:TC],
                      bosm[:, db:db + 1], None, ALU.add, rd=(pp, bosm), wr=(dst,))
                em.op("vector", "tensor_tensor", dst[:, :], dst[:, :],
                      ut[db][:, :], ALU.add, rd=(dst, ut[db]), wr=(dst,))
                em.op("vector", "tensor_tensor", dst[:, :], dst[:, :],
                      xt[db][:, :], ALU.add, rd=(dst, xt[db]), wr=(dst,))
                em.dma("gpsimd", out_d[db * 128:(db + 1) * 128, t0:t0 + TC],
                       dst[:, :], rd=(dst,))
        em.finalize()
    return nc


def build_flash(T=T):
    nc = bass.Bass("TRN2", target_bir_lowering=False)
    NH, DH, R = 16, 64, 32
    IB = T // 128
    JC = T // 512
    TC = 512
    NT = T // TC
    xT = nc.dram_tensor("xT", [D, T], F32, kind="ExternalInput")
    WqT_d = nc.dram_tensor("WqT", [D, D], F32, kind="ExternalInput")
    WkT_d = nc.dram_tensor("WkT", [D, D], F32, kind="ExternalInput")
    WvT_d = nc.dram_tensor("WvT", [D, D], F32, kind="ExternalInput")
    WoT_d = nc.dram_tensor("WoT", [D, D], F32, kind="ExternalInput")
    bq_d = nc.dram_tensor("bq", [64, NH], F32, kind="ExternalInput")
    bk_d = nc.dram_tensor("bk", [64, NH], F32, kind="ExternalInput")
    bv_d = nc.dram_tensor("bv", [1, D], F32, kind="ExternalInput")
    bo_d = nc.dram_tensor("bo", [128, KD], F32, kind="ExternalInput")
    qlsr_d = nc.dram_tensor("qlsr", [NH * 64, R], F32, kind="ExternalInput")
    klsr_d = nc.dram_tensor("klsr", [NH * 64, R], F32, kind="ExternalInput")
    mask1_d = nc.dram_tensor("mask1", [4 * 128, 512], F32, kind="ExternalInput")
    maskT_d = nc.dram_tensor("maskT", [4 * 128, 512], F32, kind="ExternalInput")
    identf_d = nc.dram_tensor("identf", [128, 128], F32, kind="ExternalInput")
    out_d = nc.dram_tensor("outT", [D, T], F32, kind="ExternalOutput")
    qlrd = nc.dram_tensor("qlrd", [NH * R, T], BF16)
    klrd = nc.dram_tensor("klrd", [NH * R, T], BF16)
    vtokd = nc.dram_tensor("vtokd", [T, D], BF16)
    yTd = nc.dram_tensor("yTd", [D, T], BF16)
    wtmp = nc.dram_tensor("wtmp", [IB, 128], F32)
    with ExitStack() as stack:
        em = Em(nc, stack)
        ones_col, ones_row = common_consts(em)
        identf = em.sb("identf_s", 128, 128, F32)
        em.dma("sync", identf[:, :], identf_d[:, :], wr=(identf,))
        stageW = em.sb("stageW", 128, D, F32)
        Wq = load_w_bf(em, "Wq", WqT_d, KD, D, stageW)
        Wk = load_w_bf(em, "Wk", WkT_d, KD, D, stageW)
        Wv = load_w_bf(em, "Wv", WvT_d, KD, D, stageW)
        Wo = load_w_bf(em, "Wo", WoT_d, KD, D, stageW)
        qlsr = em.sb("qlsr_s", 64, NH * R, BF16)
        klsr = em.sb("klsr_s", 64, NH * R, BF16)
        stage64 = em.sb("stage64", 64, R, F32)
        for h in range(NH):
            for (dd, ss) in ((qlsr_d, qlsr), (klsr_d, klsr)):
                em.dma("sync", stage64[:, :], dd[h * 64:(h + 1) * 64, :],
                       rd=(stage64,), wr=(stage64,))
                em.op("vector", "tensor_copy", ss[:, h * R:(h + 1) * R],
                      stage64[:, :], rd=(stage64,), wr=(ss,))
        masks1 = em.sb("masks1", 128, 4 * 512, F32)
        masksT = em.sb("masksT", 128, 4 * 512, F32)
        for i in range(4):
            em.dma("sync", masks1[:, i * 512:(i + 1) * 512],
                   mask1_d[i * 128:(i + 1) * 128, :], wr=(masks1,))
            em.dma("sync", masksT[:, i * 512:(i + 1) * 512],
                   maskT_d[i * 128:(i + 1) * 128, :], wr=(masksT,))
        bsm = em.sb("bsm", 64, 2 * NH, F32)
        em.dma("sync", bsm[:, 0:NH], bq_d[:, :], wr=(bsm,))
        em.dma("sync", bsm[:, NH:2 * NH], bk_d[:, :], wr=(bsm,))
        bvrow = em.sb("bvrow", 1, D, F32)
        em.dma("sync", bvrow[0:1, :], bv_d[:, :], wr=(bvrow,))
        bvb = em.sb("bvb", 128, D, F32)
        ps_a = em.ps("ps_a", 128, 512)
        ps_b = em.ps("ps_b", 128, 512)
        ps_s1 = em.ps("ps_s1", 1, 512)
        ps_s2 = em.ps("ps_s2", 1, 512)
        ps_q = em.ps("ps_q", 64, 512)
        ps_r = em.ps("ps_r", 32, 512)
        ps_o = em.ps("ps_o", 64, 512)
        sqb = [em.sb(f"lnsq{i}", 128, TC, F32) for i in range(2)]
        rows = [em.sb(f"lnrow{i}", 1, TC, F32) for i in range(4)]
        lns = (ps_s1, ps_s2, ps_a, ps_b, sqb, rows)
        for half in range(2):
            em.op("tensor", "matmul", ps_a[:, 0:512], ones_row[0:1, :],
                  bvrow[0:1, half * 512:(half + 1) * 512],
                  rd=(ones_row, bvrow), wr=(ps_a,), kw=dict(start=True, stop=True))
            em.op("scalar", "copy", bvb[:, half * 512:(half + 1) * 512],
                  ps_a[:, 0:512], rd=(ps_a,), wr=(bvb,))
        xt = [em.sb(f"xt{k}", 128, TC, F32) for k in range(KD)]
        ut = [em.sb(f"ut{k}", 128, TC, F32) for k in range(KD)]
        yn = [em.sb(f"yn{k}", 128, TC, BF16) for k in range(KD)]
        qst = em.sb("qst", 64, TC, BF16)
        qlst = em.sb("qlst", 32, TC, BF16)
        vstage = em.sb("vstage", 128, 512, BF16)
        for c in range(NT):
            t0 = c * TC
            for kb in range(KD):
                em.dma("sync", xt[kb][:, :], xT[kb * 128:(kb + 1) * 128, t0:t0 + TC],
                       rd=(xt[kb],), wr=(xt[kb],))
            emit_ln(em, xt, ut, yn, TC, *lns, ones_col, ones_row)
            for h in range(NH):
                for (Wsb, bcol, lsr, dstd) in ((Wq, 0, qlsr, qlrd),
                                               (Wk, NH, klsr, klrd)):
                    for kb in range(KD):
                        em.op("tensor", "matmul", ps_q[:, 0:TC],
                              Wsb[kb][:, h * 64:(h + 1) * 64], yn[kb][:, 0:TC],
                              rd=(Wsb[kb], yn[kb]), wr=(ps_q,),
                              kw=dict(start=(kb == 0), stop=(kb == KD - 1)))
                    em.op("vector", "tensor_scalar", qst[:, 0:TC], ps_q[:, 0:TC],
                          bsm[:, bcol + h:bcol + h + 1], None, ALU.add,
                          rd=(ps_q, bsm), wr=(qst,))
                    em.op("tensor", "matmul", ps_r[:, 0:TC],
                          lsr[:, h * R:(h + 1) * R], qst[:, 0:TC],
                          rd=(lsr, qst), wr=(ps_r,), kw=dict(start=True, stop=True))
                    em.op("scalar", "copy", qlst[:, 0:TC], ps_r[0:32, 0:TC],
                          rd=(ps_r,), wr=(qlst,))
                    em.dma("gpsimd", dstd[h * R:(h + 1) * R, t0:t0 + TC],
                           qlst[:, 0:TC], rd=(qlst,), wr=(dstd,))
            for js in range(TC // 128):
                for half in range(2):
                    for kb in range(KD):
                        em.op("tensor", "matmul", ps_b[:, 0:512],
                              yn[kb][:, js * 128:(js + 1) * 128],
                              Wv[kb][:, half * 512:(half + 1) * 512],
                              rd=(yn[kb], Wv[kb]), wr=(ps_b,),
                              kw=dict(start=(kb == 0), stop=(kb == KD - 1)))
                    em.op("vector", "tensor_tensor", vstage[:, :], ps_b[:, :],
                          bvb[:, half * 512:(half + 1) * 512], ALU.add,
                          rd=(ps_b, bvb), wr=(vstage,))
                    em.dma("gpsimd",
                           vtokd[t0 + js * 128:t0 + (js + 1) * 128,
                                 half * 512:(half + 1) * 512],
                           vstage[:, :], rd=(vstage,), wr=(vtokd,))
        em.barrier()
        qlt = [em.sb(f"qlt{i}", 32, 128, BF16) for i in range(2)]
        klt = [em.sb(f"klt{i}", 32, 512, BF16) for i in range(2)]
        vtb = [em.sb(f"vtb{i}", 128, 64, BF16) for i in range(2)]
        acc = em.sb("acc", 128, JC, F32)
        lall = em.sb("lall", 128, IB, F32)
        lrec = em.sb("lrec", 128, IB, F32)
        sfb = em.sb("sfb", 128, 512, F32)
        ptb = em.sb("ptb", 128, 512, BF16)
        ystg = em.sb("ystg", 64, 512, F32)
        ystb = em.sb("ystb", 64, 512, BF16)
        wrow = em.sb("wrow", 1, T, F32)
        trow = em.sb("trow", 32, 128, F32)
        scale = 1.0 / np.sqrt(32.0)
        for h in range(NH):
            for ib in range(IB):
                nch = ib // 4 + 1
                em.dma("sync", qlt[ib % 2][:, :],
                       qlrd[h * R:(h + 1) * R, ib * 128:(ib + 1) * 128],
                       rd=(qlt[ib % 2], qlrd), wr=(qlt[ib % 2],))
                for jc in range(nch):
                    kt_ = klt[jc % 2]
                    em.dma("sync", kt_[:, :],
                           klrd[h * R:(h + 1) * R, jc * 512:(jc + 1) * 512],
                           rd=(kt_, klrd), wr=(kt_,))
                    em.op("tensor", "matmul", ps_a[:, 0:512], qlt[ib % 2][:, :],
                          kt_[:, :], rd=(qlt[ib % 2], kt_), wr=(ps_a,),
                          kw=dict(start=True, stop=True))
                    if jc == ib // 4:
                        off = ib % 4
                        em.op("vector", "scalar_tensor_tensor", sfb[:, :],
                              ps_a[:, 0:512], scale,
                              masks1[:, off * 512:(off + 1) * 512],
                              ALU.mult, ALU.add, rd=(ps_a, masks1), wr=(sfb,))
                        em.op("scalar", "activation", ptb[:, :], sfb[:, :], AF.Exp,
                              rd=(sfb,), wr=(ptb, acc),
                              kw=dict(accum_out=acc[:, jc:jc + 1]))
                    else:
                        em.op("scalar", "activation", ptb[:, :], ps_a[:, 0:512],
                              AF.Exp, rd=(ps_a,), wr=(ptb, acc),
                              kw=dict(scale=scale, accum_out=acc[:, jc:jc + 1]))
                em.op("vector", "reduce_sum", lall[:, ib:ib + 1], acc[:, 0:nch],
                      rd=(acc,), wr=(lall,), kw=dict(axis=AX.X))
            em.op("vector", "reciprocal", lrec[:, :], lall[:, :],
                  rd=(lall,), wr=(lrec,))
            em.op("tensor", "transpose", ps_b[0:IB, 0:128], lrec[:, 0:IB],
                  identf[:, :], rd=(lrec, identf), wr=(ps_b,))
            em.op("scalar", "copy", trow[0:IB, 0:128], ps_b[0:IB, 0:128],
                  rd=(ps_b,), wr=(trow,))
            em.dma("gpsimd", wtmp[:, :], trow[0:IB, 0:128], rd=(trow,), wr=(wtmp,))
            em.dma("sync", wrow[0:1, :], bass.AP(wtmp, 0, [[0, 1], [1, T]]),
                   rd=(wtmp,), wr=(wrow,))
            for tqc in range(JC):
                em.op("tensor", "matmul", ps_q[:, 0:512], ones_row[0:1, 0:64],
                      wrow[0:1, tqc * 512:(tqc + 1) * 512],
                      rd=(ones_row, wrow), wr=(ps_q,), kw=dict(start=True, stop=True))
                em.dma("sync", klt[0][:, :],
                       qlrd[h * R:(h + 1) * R, tqc * 512:(tqc + 1) * 512],
                       rd=(klt[0], qlrd), wr=(klt[0],))
                for tkb in range(4 * tqc + 4):
                    em.dma("sync", qlt[tkb % 2][:, :],
                           klrd[h * R:(h + 1) * R, tkb * 128:(tkb + 1) * 128],
                           rd=(qlt[tkb % 2], klrd), wr=(qlt[tkb % 2],))
                    em.op("tensor", "matmul", ps_a[:, 0:512], qlt[tkb % 2][:, :],
                          klt[0][:, :], rd=(qlt[tkb % 2], klt[0]), wr=(ps_a,),
                          kw=dict(start=True, stop=True))
                    if tkb // 4 == tqc:
                        off = tkb % 4
                        em.op("vector", "scalar_tensor_tensor", sfb[:, :],
                              ps_a[:, 0:512], scale,
                              masksT[:, off * 512:(off + 1) * 512],
                              ALU.mult, ALU.add, rd=(ps_a, masksT), wr=(sfb,))
                        em.op("scalar", "activation", ptb[:, :], sfb[:, :],
                              AF.Exp, rd=(sfb,), wr=(ptb,))
                    else:
                        em.op("scalar", "activation", ptb[:, :], ps_a[:, 0:512],
                              AF.Exp, rd=(ps_a,), wr=(ptb,), kw=dict(scale=scale))
                    em.dma("sync", vtb[tkb % 2][:, :],
                           vtokd[tkb * 128:(tkb + 1) * 128, h * 64:(h + 1) * 64],
                           rd=(vtb[tkb % 2], vtokd), wr=(vtb[tkb % 2],))
                    em.op("tensor", "matmul", ps_o[:, 0:512], vtb[tkb % 2][:, :],
                          ptb[:, :], rd=(vtb[tkb % 2], ptb), wr=(ps_o,),
                          kw=dict(start=(tkb == 0), stop=(tkb == 4 * tqc + 3)))
                em.op("scalar", "copy", ystg[:, :], ps_o[:, 0:512],
                      rd=(ps_o,), wr=(ystg,))
                em.op("vector", "tensor_tensor", ystb[:, :], ystg[:, :],
                      ps_q[0:64, 0:512], ALU.mult, rd=(ystg, ps_q), wr=(ystb,))
                em.dma("gpsimd", yTd[h * 64:(h + 1) * 64,
                                     tqc * 512:(tqc + 1) * 512],
                       ystb[:, :], rd=(ystb,), wr=(yTd,))
        em.barrier()
        bosm = em.sb("bosm", 128, KD, F32)
        em.dma("sync", bosm[:, :], bo_d[:, :], wr=(bosm,))
        ytk = [em.sb(f"ytk{k}", 128, TC, BF16) for k in range(KD)]
        o1 = [em.sb(f"o1f{k}", 128, TC, F32) for k in range(2)]
        for c in range(NT):
            t0 = c * TC
            for kb in range(KD):
                em.dma("sync", xt[kb][:, :], xT[kb * 128:(kb + 1) * 128, t0:t0 + TC],
                       rd=(xt[kb],), wr=(xt[kb],))
                em.dma("sync", ytk[kb][:, :], yTd[kb * 128:(kb + 1) * 128, t0:t0 + TC],
                       rd=(ytk[kb], yTd), wr=(ytk[kb],))
            emit_ln(em, xt, ut, None, TC, *lns, ones_col, ones_row)
            for db in range(KD):
                pp = ps_a if db % 2 == 0 else ps_b
                for kb in range(KD):
                    em.op("tensor", "matmul", pp[:, 0:TC],
                          Wo[kb][:, db * 128:(db + 1) * 128], ytk[kb][:, 0:TC],
                          rd=(Wo[kb], ytk[kb]), wr=(pp,),
                          kw=dict(start=(kb == 0), stop=(kb == KD - 1)))
                dst = o1[db % 2]
                em.op("vector", "tensor_scalar", dst[:, :], pp[:, 0:TC],
                      bosm[:, db:db + 1], None, ALU.add, rd=(pp, bosm), wr=(dst,))
                em.op("vector", "tensor_tensor", dst[:, :], dst[:, :],
                      ut[db][:, :], ALU.add, rd=(dst, ut[db]), wr=(dst,))
                em.op("vector", "tensor_tensor", dst[:, :], dst[:, :],
                      xt[db][:, :], ALU.add, rd=(dst, xt[db]), wr=(dst,))
                em.dma("gpsimd", out_d[db * 128:(db + 1) * 128, t0:t0 + TC],
                       dst[:, :], rd=(dst,))
        em.finalize()
    return nc


def build_moe(T=T):
    nc = bass.Bass("TRN2", target_bir_lowering=False)
    E_, S_, FH = 8, 2, 2048
    KH = FH // 128
    TH = T // 2
    TC = min(512, TH)
    NCH = TH // TC
    NEX = S_ + E_
    xT = nc.dram_tensor("xT", [D, T], F32, kind="ExternalInput")
    rw_d = nc.dram_tensor("rw", [D, E_], F32, kind="ExternalInput")
    rb_d = nc.dram_tensor("rb", [E_, 1], F32, kind="ExternalInput")
    w1_d = nc.dram_tensor("w1", [NEX * D, FH], F32, kind="ExternalInput")
    b1_d = nc.dram_tensor("b1", [128, NEX * KH], F32, kind="ExternalInput")
    w2_d = nc.dram_tensor("w2", [NEX * FH, D], F32, kind="ExternalInput")
    b2_d = nc.dram_tensor("b2", [128, E_ * KD], F32, kind="ExternalInput")
    sb2_d = nc.dram_tensor("sb2", [128, S_ * KD], F32, kind="ExternalInput")
    identf_d = nc.dram_tensor("identf", [128, 128], F32, kind="ExternalInput")
    out_d = nc.dram_tensor("outT", [D, T], F32, kind="ExternalOutput")
    w1bf = nc.dram_tensor("w1bf", [NEX * D, FH], BF16)
    w2bf = nc.dram_tensor("w2bf", [NEX * FH, D], BF16)
    combd = nc.dram_tensor("combd", [E_, T], F32)
    und = nc.dram_tensor("und", [D, T], BF16)
    with ExitStack() as stack:
        em = Em(nc, stack)
        ones_col, ones_row = common_consts(em)
        identf = em.sb("identf_s", 128, 128, F32)
        em.dma("sync", identf[:, :], identf_d[:, :], wr=(identf,))
        stg = em.sb("stg", 128, 2 * FH, F32)
        stgb = em.sb("stgb", 128, FH, BF16)
        for r in range(NEX * KD):
            em.dma("sync", stg[:, 0:FH], w1_d[r * 128:(r + 1) * 128, :],
                   rd=(stg,), wr=(stg,))
            em.op("vector", "tensor_copy", stgb[:, :], stg[:, 0:FH], rd=(stg,), wr=(stgb,))
            em.dma("gpsimd", w1bf[r * 128:(r + 1) * 128, :], stgb[:, :],
                   rd=(stgb,), wr=(w1bf,))
        for r in range(NEX * KH):
            em.dma("sync", stg[:, 0:D], w2_d[r * 128:(r + 1) * 128, :],
                   rd=(stg,), wr=(stg,))
            em.op("vector", "tensor_copy", stgb[:, 0:D], stg[:, 0:D],
                  rd=(stg,), wr=(stgb,))
            em.dma("gpsimd", w2bf[r * 128:(r + 1) * 128, :], stgb[:, 0:D],
                  rd=(stgb,), wr=(w2bf,))
        em.barrier()
        small = em.sb("small", 128, NEX * KH + E_ * KD + S_ * KD + KD, F32)
        C_B1, C_B2, C_SB2, C_SS = 0, NEX * KH, NEX * KH + E_ * KD, NEX * KH + E_ * KD + S_ * KD
        em.dma("sync", small[:, C_B1:C_B1 + NEX * KH], b1_d[:, :], wr=(small,))
        em.dma("sync", small[:, C_B2:C_B2 + E_ * KD], b2_d[:, :], wr=(small,))
        em.dma("sync", small[:, C_SB2:C_SB2 + S_ * KD], sb2_d[:, :], wr=(small,))
        em.op("vector", "tensor_add", small[:, C_SS:C_SS + KD],
              small[:, C_SB2:C_SB2 + KD], small[:, C_SB2 + KD:C_SB2 + 2 * KD],
              rd=(small,), wr=(small,))
        em.op("vector", "tensor_scalar_mul", small[:, C_SS:C_SS + KD],
              small[:, C_SS:C_SS + KD], 0.5, rd=(small,), wr=(small,))
        rwsb = []
        for kb in range(KD):
            t_ = em.sb(f"rwf{kb}", 128, E_, F32)
            em.dma("sync", t_[:, :], rw_d[kb * 128:(kb + 1) * 128, :], wr=(t_,))
            rwsb.append(t_)
        rbs = em.sb("rbs", E_, 1, F32)
        em.dma("sync", rbs[:, :], rb_d[:, :], wr=(rbs,))
        ps_a = em.ps("ps_a", 128, 512)
        ps_b = em.ps("ps_b", 128, 512)
        ps_s1 = em.ps("ps_s1", 1, 512)
        ps_s2 = em.ps("ps_s2", 1, 512)
        ps_g = em.ps("ps_g", 8, 512)
        ps_t = em.ps("ps_t", 128, 128)
        ps_w = em.ps("ps_w", 128, 512)
        sqb = [em.sb(f"lnsq{i}", 128, TC, F32) for i in range(2)]
        rows = [em.sb(f"lnrow{i}", 1, TC, F32) for i in range(4)]
        lns = (ps_s1, ps_s2, ps_a, ps_b, sqb, rows)
        xt = [em.sb(f"xt{k}", 128, TC, F32) for k in range(KD)]
        ut = [stg[:, k * TC:(k + 1) * TC] for k in range(KD)]
        ynh = [em.sb(f"ynh{k}", 128, TC, BF16) for k in range(KD)]
        gts = em.sb("gts", 8, TH, F32)
        gtok = em.sb("gtok", 128, 8, F32)
        m1 = em.sb("m1", 128, 1, F32)
        m2 = em.sb("m2", 128, 1, F32)
        mk1 = em.sb("mk1", 128, 8, F32)
        mk2 = em.sb("mk2", 128, 8, F32)
        msk = em.sb("msk", 128, 8, F32)
        p1 = em.sb("p1", 128, 1, F32)
        p2 = em.sb("p2", 128, 1, F32)
        wec = em.sb("wec", 128, 8, F32)
        cstage = em.sb("cstage", 8, 128, F32)
        crow = em.sb("crow", 1, TH, F32)
        w1sb = [em.sb(f"w1sb{k}", 128, FH, BF16) for k in range(KD)]
        hx = [em.sb(f"hx{m}", 128, TC, BF16) for m in range(KH)]
        w2t = [em.sb(f"w2t{i}", 128, D, BF16) for i in range(2)]
        accs = [em.sb(f"accs{k}", 128, TH, F32) for k in range(KD)]
        tstage = em.sb("tstage", 128, 512, F32)
        tm = em.sb("tm", 128, 512, F32)
        o1 = [tstage, tm]
        for half in range(2):
            h0 = half * TH
            for c in range(NCH):
                t0 = h0 + c * TC
                for kb in range(KD):
                    em.dma("sync", xt[kb][:, :],
                           xT[kb * 128:(kb + 1) * 128, t0:t0 + TC],
                           rd=(xt[kb],), wr=(xt[kb],))
                emit_ln(em, xt, ut, None, TC, *lns, ones_col, ones_row)
                for kb in range(KD):
                    em.op("vector", "tensor_copy", ynh[kb][:, :],
                          ut[kb][:, :], rd=(ut[kb],), wr=(ynh[kb],))
                    em.dma("gpsimd", und[kb * 128:(kb + 1) * 128, t0:t0 + TC],
                           ynh[kb][:, :], rd=(ynh[kb],), wr=(und,))
                for kb in range(KD):
                    em.op("tensor", "matmul", ps_g[:, 0:TC], rwsb[kb][:, 0:E_],
                          ut[kb][:, :],
                          rd=(rwsb[kb], ut[kb]), wr=(ps_g,),
                          kw=dict(start=(kb == 0), stop=(kb == KD - 1)))
                em.op("vector", "tensor_scalar", gts[:, c * TC:(c + 1) * TC],
                      ps_g[0:8, 0:TC], rbs[:, 0:1], None, ALU.add,
                      rd=(ps_g, rbs), wr=(gts,))
                for tb in range(TC // 128):
                    em.op("tensor", "transpose", ps_t[:, 0:8],
                          gts[0:8, c * TC + tb * 128:c * TC + (tb + 1) * 128],
                          identf[0:8, 0:8], rd=(gts, identf), wr=(ps_t,))
                    em.op("scalar", "copy", gtok[:, 0:8], ps_t[:, 0:8],
                          rd=(ps_t,), wr=(gtok,))
                    em.op("vector", "reduce_max", m1[:, :], gtok[:, 0:8],
                          rd=(gtok,), wr=(m1,), kw=dict(axis=AX.X))
                    em.op("vector", "tensor_scalar", mk1[:, :], gtok[:, 0:8],
                          m1[:, 0:1], None, ALU.is_equal, rd=(gtok, m1), wr=(mk1,))
                    em.op("vector", "scalar_tensor_tensor", msk[:, :], mk1[:, :],
                          -1e9, gtok[:, 0:8], ALU.mult, ALU.add,
                          rd=(mk1, gtok), wr=(msk,))
                    em.op("vector", "reduce_max", m2[:, :], msk[:, :],
                          rd=(msk,), wr=(m2,), kw=dict(axis=AX.X))
                    em.op("vector", "tensor_scalar", mk2[:, :], gtok[:, 0:8],
                          m2[:, 0:1], None, ALU.is_equal, rd=(gtok, m2), wr=(mk2,))
                    em.op("vector", "tensor_tensor", p1[:, :], m1[:, :], m2[:, :],
                          ALU.subtract, rd=(m1, m2), wr=(p1,))
                    em.op("scalar", "activation", p1[:, :], p1[:, :], AF.Sigmoid,
                          rd=(p1,), wr=(p1,))
                    em.op("vector", "tensor_scalar", p2[:, :], p1[:, :], 1.0, -1.0,
                          ALU.subtract, ALU.mult, rd=(p1,), wr=(p2,))
                    em.op("vector", "tensor_scalar", wec[:, :], mk1[:, :],
                          p1[:, 0:1], None, ALU.mult, rd=(mk1, p1), wr=(wec,))
                    em.op("vector", "scalar_tensor_tensor", wec[:, :], mk2[:, :],
                          p2[:, 0:1], wec[:, :], ALU.mult, ALU.add,
                          rd=(mk2, p2, wec), wr=(wec,))
                    em.op("tensor", "transpose", ps_t[0:8, 0:128], wec[:, :],
                          identf[:, :], rd=(wec, identf), wr=(ps_t,))
                    em.op("scalar", "copy", cstage[:, :],
                          ps_t[0:8, 0:128], rd=(ps_t,), wr=(cstage,))
                    em.dma("gpsimd",
                           combd[:, h0 + c * TC + tb * 128:h0 + c * TC + (tb + 1) * 128],
                           cstage[:, :], rd=(cstage,), wr=(combd,))
            for kb in range(KD):
                em.op("vector", "memset", accs[kb][:, :], 0.0, wr=(accs[kb],))
            for ex in range(NEX):
                for kb in range(KD):
                    em.dma("sync", w1sb[kb][:, :],
                           w1bf[(ex * KD + kb) * 128:(ex * KD + kb + 1) * 128, :],
                           rd=(w1sb[kb], w1bf), wr=(w1sb[kb],))
                if ex >= S_:
                    em.dma("sync", crow[0:1, :],
                           combd[ex - S_:ex - S_ + 1, h0:h0 + TH],
                           rd=(crow, combd), wr=(crow,))
                for c in range(NCH):
                    for kb in range(KD):
                        em.dma("sync", ynh[kb][:, :],
                               und[kb * 128:(kb + 1) * 128,
                                   h0 + c * TC:h0 + (c + 1) * TC],
                               rd=(ynh[kb], und), wr=(ynh[kb],))
                    for mb in range(KH):
                        pp = ps_a if mb % 2 == 0 else ps_b
                        for kb in range(KD):
                            em.op("tensor", "matmul", pp[:, 0:TC],
                                  w1sb[kb][:, mb * 128:(mb + 1) * 128],
                                  ynh[kb][:, :],
                                  rd=(w1sb[kb], ynh[kb]), wr=(pp,),
                                  kw=dict(start=(kb == 0), stop=(kb == KD - 1)))
                        em.op("scalar", "activation", hx[mb][:, 0:TC], pp[:, 0:TC],
                              AF.Gelu, rd=(pp, small), wr=(hx[mb],),
                              kw=dict(bias=small[:, C_B1 + ex * KH + mb:
                                                 C_B1 + ex * KH + mb + 1]))
                    if ex >= S_:
                        em.op("tensor", "matmul", ps_w[:, 0:TC], ones_row[0:1, :],
                              crow[0:1, c * TC:(c + 1) * TC],
                              rd=(ones_row, crow), wr=(ps_w,),
                              kw=dict(start=True, stop=True))
                    for db in range(KD):
                        pp = ps_a if db % 2 == 0 else ps_b
                        for hb in range(KH):
                            wt = w2t[hb % 2]
                            em.dma("sync", wt[:, :],
                                   w2bf[(ex * KH + hb) * 128:(ex * KH + hb + 1) * 128, :],
                                   rd=(wt, w2bf), wr=(wt,))
                            em.op("tensor", "matmul", pp[:, 0:TC],
                                  wt[:, db * 128:(db + 1) * 128], hx[hb][:, 0:TC],
                                  rd=(wt, hx[hb]), wr=(pp,),
                                  kw=dict(start=(hb == 0), stop=(hb == KH - 1)))
                        aslice = accs[db][:, c * TC:(c + 1) * TC]
                        if ex < S_:
                            em.op("vector", "scalar_tensor_tensor", aslice,
                                  pp[:, 0:TC], 0.5, aslice, ALU.mult, ALU.add,
                                  rd=(pp, accs[db]), wr=(accs[db],))
                        else:
                            em.op("vector", "tensor_scalar", tstage[:, 0:TC],
                                  pp[:, 0:TC],
                                  small[:, C_B2 + (ex - S_) * KD + db:
                                         C_B2 + (ex - S_) * KD + db + 1],
                                  None, ALU.add, rd=(pp, small), wr=(tstage,))
                            em.op("vector", "tensor_tensor", tm[:, 0:TC],
                                  tstage[:, 0:TC], ps_w[:, 0:TC], ALU.mult,
                                  rd=(tstage, ps_w), wr=(tm,))
                            em.op("vector", "tensor_tensor", aslice, aslice,
                                  tm[:, 0:TC], ALU.add, rd=(accs[db], tm),
                                  wr=(accs[db],))
            for c in range(NCH):
                t0 = h0 + c * TC
                for kb in range(KD):
                    em.dma("sync", xt[kb][:, :],
                           xT[kb * 128:(kb + 1) * 128, t0:t0 + TC],
                           rd=(xt[kb],), wr=(xt[kb],))
                for db in range(KD):
                    dst = o1[db % 2]
                    em.op("vector", "tensor_scalar", dst[:, 0:TC],
                          accs[db][:, c * TC:(c + 1) * TC],
                          small[:, C_SS + db:C_SS + db + 1], None, ALU.add,
                          rd=(accs[db], small), wr=(dst,))
                    em.op("vector", "tensor_tensor", dst[:, 0:TC], dst[:, 0:TC],
                          xt[db][:, :], ALU.add, rd=(dst, xt[db]), wr=(dst,))
                    em.dma("gpsimd", out_d[db * 128:(db + 1) * 128, t0:t0 + TC],
                           dst[:, 0:TC], rd=(dst,))
        em.finalize()
    return nc


def build_liquid(T=T):
    nc = bass.Bass("TRN2", target_bir_lowering=False)
    DIN, N_ = 2048, 16
    KI = DIN // 128
    TC = 512
    NT = T // TC
    NTILE = DIN * N_ // 128
    xT = nc.dram_tensor("xT", [D, T], F32, kind="ExternalInput")
    nw_d = nc.dram_tensor("nw", [128, KD], F32, kind="ExternalInput")
    ip_d = nc.dram_tensor("ipw", [D, 2 * DIN], F32, kind="ExternalInput")
    ap_d = nc.dram_tensor("apw", [DIN, DIN], F32, kind="ExternalInput")
    apb_d = nc.dram_tensor("apb", [128, KI], F32, kind="ExternalInput")
    abase_d = nc.dram_tensor("abase", [128, 1], F32, kind="ExternalInput")
    bp_d = nc.dram_tensor("bpw", [DIN, N_], F32, kind="ExternalInput")
    cp_d = nc.dram_tensor("cpw", [DIN, N_], F32, kind="ExternalInput")
    dvec_d = nc.dram_tensor("dvec", [128, KI], F32, kind="ExternalInput")
    op_d = nc.dram_tensor("opw", [DIN, D], F32, kind="ExternalInput")
    selb_d = nc.dram_tensor("selb", [16 * 128, 128], BF16, kind="ExternalInput")
    out_d = nc.dram_tensor("outT", [D, T], F32, kind="ExternalOutput")
    xind = nc.dram_tensor("xind", [DIN, T], F32)
    zd = nc.dram_tensor("zd", [DIN, T], F32)
    ad = nc.dram_tensor("ad", [DIN, T], F32)
    btd = nc.dram_tensor("btd", [N_, T], F32)
    ctd = nc.dram_tensor("ctd", [N_, T], F32)
    yd = nc.dram_tensor("yd", [DIN, T], F32)
    with ExitStack() as stack:
        em = Em(nc, stack)
        ones_col, ones_row = common_consts(em)
        PF = em.sb("PF", 128, 28672, F32)
        PB = em.sb("PB", 128, 40960, BF16)
        LNS = em.sb("LNS", 128, 2560, F32)

        small = em.sb("smalll", 128, KD + KI + 1 + KI, F32)
        C_NW, C_APB, C_AB, C_DV = 0, KD, KD + KI, KD + KI + 1
        em.dma("sync", small[:, C_NW:C_NW + KD], nw_d[:, :], wr=(small,))
        em.dma("sync", small[:, C_APB:C_APB + KI], apb_d[:, :], wr=(small,))
        em.dma("sync", small[:, C_AB:C_AB + 1], abase_d[:, :], wr=(small,))
        em.dma("sync", small[:, C_DV:C_DV + KI], dvec_d[:, :], wr=(small,))
        em.op("scalar", "activation", small[:, C_AB:C_AB + 1],
              small[:, C_AB:C_AB + 1], AF.Sigmoid, rd=(small,), wr=(small,))
        ps_a = em.ps("ps_a", 128, 512)
        ps_b = em.ps("ps_b", 128, 512)
        ps_s1 = em.ps("ps_s1", 1, 512)
        ps_s2 = em.ps("ps_s2", 1, 512)
        ps_y = [em.ps(f"ps_y{i}", 128, 512) for i in range(2)]
        ps_bc = em.ps("ps_bc", 16, 512)
        sqb = [LNS[:, 0:512], LNS[:, 512:1024]]
        rows = [LNS[0:1, 1024:1536], LNS[0:1, 1536:2048],
                LNS[0:1, 2048:2560], LNS[0:1, 1536:2048]]
        lns = (ps_s1, ps_s2, ps_a, ps_b, sqb, rows)
        xt = [PF[:, k * 512:(k + 1) * 512] for k in range(KD)]
        ut = [PF[:, 4096 + k * 512:4096 + (k + 1) * 512] for k in range(KD)]
        st1 = PF[:, 8192:8704]
        yn = [PB[:, k * 512:(k + 1) * 512] for k in range(KD)]
        ipsb = [PB[:, 4096 + k * 4096:4096 + (k + 1) * 4096] for k in range(KD)]
        stgf = PF[:, 8704:8704 + 4096]
        for kb in range(KD):
            em.dma("sync", stgf, ip_d[kb * 128:(kb + 1) * 128, :], rd=(stgf,), wr=(stgf,))
            em.op("vector", "tensor_copy", ipsb[kb], stgf, rd=(stgf,), wr=(ipsb[kb],))
        for c in range(NT):
            t0 = c * TC
            for kb in range(KD):
                em.dma("sync", xt[kb], xT[kb * 128:(kb + 1) * 128, t0:t0 + TC],
                       rd=(xt[kb],), wr=(xt[kb],))
            emit_ln(em, xt, ut, None, TC, *lns, ones_col, ones_row)
            for kb in range(KD):
                em.op("vector", "tensor_scalar", yn[kb], ut[kb],
                      small[:, C_NW + kb:C_NW + kb + 1], None, ALU.mult,
                      rd=(ut[kb], small), wr=(yn[kb],))
            for mb in range(2 * KI):
                pp = ps_a if mb % 2 == 0 else ps_b
                for kb in range(KD):
                    em.op("tensor", "matmul", pp[:, 0:TC],
                          ipsb[kb][:, mb * 128:(mb + 1) * 128], yn[kb],
                          rd=(ipsb[kb], yn[kb]), wr=(pp,),
                          kw=dict(start=(kb == 0), stop=(kb == KD - 1)))
                if mb < KI:
                    em.op("scalar", "activation", st1, pp[:, 0:TC], AF.Silu,
                          rd=(pp,), wr=(st1,))
                    em.dma("gpsimd", xind[mb * 128:(mb + 1) * 128, t0:t0 + TC],
                           st1, rd=(st1,), wr=(xind,))
                else:
                    em.op("scalar", "copy", st1, pp[:, 0:TC], rd=(pp,), wr=(st1,))
                    em.dma("gpsimd", zd[(mb - KI) * 128:(mb - KI + 1) * 128,
                                        t0:t0 + TC], st1, rd=(st1,), wr=(zd,))
        em.barrier()
        apsb = [PB[:, k * 2048:(k + 1) * 2048] for k in range(KI)]
        xbk = [PB[:, 32768 + k * 512:32768 + (k + 1) * 512] for k in range(KI)]
        xck = [PF[:, k * 512:(k + 1) * 512] for k in range(KI)]
        ast = PF[:, 8192:8704]
        stgf2 = PF[:, 8704:8704 + 2048]
        for kb in range(KI):
            em.dma("sync", stgf2, ap_d[kb * 128:(kb + 1) * 128, :], rd=(stgf2,), wr=(stgf2,))
            em.op("vector", "tensor_copy", apsb[kb], stgf2, rd=(stgf2,), wr=(apsb[kb],))
        bpsb = em.sb("bpsb", 128, KI * N_, BF16)
        cpsb = em.sb("cpsb", 128, KI * N_, BF16)
        stgs = em.sb("stgs", 128, N_, F32)
        for kb in range(KI):
            em.dma("sync", stgs[:, :], bp_d[kb * 128:(kb + 1) * 128, :],
                   rd=(stgs,), wr=(stgs,))
            em.op("vector", "tensor_copy", bpsb[:, kb * N_:(kb + 1) * N_],
                  stgs[:, :], rd=(stgs,), wr=(bpsb,))
            em.dma("sync", stgs[:, :], cp_d[kb * 128:(kb + 1) * 128, :],
                   rd=(stgs,), wr=(stgs,))
            em.op("vector", "tensor_copy", cpsb[:, kb * N_:(kb + 1) * N_],
                  stgs[:, :], rd=(stgs,), wr=(cpsb,))
        btr = LNS[0:16, 0:512]
        for c in range(NT):
            t0 = c * TC
            for kb in range(KI):
                em.dma("sync", xck[kb], xind[kb * 128:(kb + 1) * 128, t0:t0 + TC],
                       rd=(xck[kb],), wr=(xck[kb],))
                em.op("vector", "tensor_copy", xbk[kb], xck[kb],
                      rd=(xck[kb],), wr=(xbk[kb],))
            for mb in range(KI):
                pp = ps_a if mb % 2 == 0 else ps_b
                for kb in range(KI):
                    em.op("tensor", "matmul", pp[:, 0:TC],
                          apsb[kb][:, mb * 128:(mb + 1) * 128], xbk[kb],
                          rd=(apsb[kb], xbk[kb]), wr=(pp,),
                          kw=dict(start=(kb == 0), stop=(kb == KI - 1)))
                em.op("scalar", "activation", ast, pp[:, 0:TC], AF.Sigmoid,
                      rd=(pp, small), wr=(ast,),
                      kw=dict(bias=small[:, C_APB + mb:C_APB + mb + 1]))
                em.dma("gpsimd", ad[mb * 128:(mb + 1) * 128, t0:t0 + TC],
                       ast, rd=(ast,), wr=(ad,))
            for (wsb, dd) in ((bpsb, btd), (cpsb, ctd)):
                for kb in range(KI):
                    em.op("tensor", "matmul", ps_bc[:, 0:TC],
                          wsb[:, kb * N_:(kb + 1) * N_], xbk[kb],
                          rd=(wsb, xbk[kb]), wr=(ps_bc,),
                          kw=dict(start=(kb == 0), stop=(kb == KI - 1)))
                em.op("scalar", "copy", btr[0:16, 0:TC], ps_bc[0:16, 0:TC],
                      rd=(ps_bc,), wr=(btr,))
                em.dma("gpsimd", dd[:, t0:t0 + TC], btr[0:16, 0:TC],
                       rd=(btr,), wr=(dd,))
        em.barrier()
        b_rep = PF[:, 0:T]
        c_rep = PF[:, T:2 * T]
        _ar = PF[:, 2 * T:3 * T]
        _xr = PF[:, 3 * T:4 * T]
        a_rep = [_ar, _ar]
        x_rep = [_xr, _xr]
        hbuf = PF[:, 4 * T:5 * T]
        yacc = PF[:, 5 * T:6 * T]
        prod = PB[:, 0:T]
        selb = PB[:, T:T + 2048]
        em.dma("sync", selb, bass.AP(selb_d, 0, [[128, 128], [16384, 16], [1, 128]]),
               wr=(selb,))
        em.dma("sync", b_rep, bass.AP(btd, 0, [[0, 8], [T, 16], [1, T]]),
               rd=(btd,), wr=(b_rep,))
        em.dma("sync", c_rep, bass.AP(ctd, 0, [[0, 8], [T, 16], [1, T]]),
               rd=(ctd,), wr=(c_rep,))
        for kt in range(NTILE):
            ar = a_rep[kt % 2]
            xr = x_rep[kt % 2]
            em.dma("sync", ar, bass.AP(ad, 8 * kt * T, [[T, 8], [0, 16], [1, T]]),
                   rd=(ar, ad), wr=(ar,))
            em.dma("sync", xr, bass.AP(xind, 8 * kt * T, [[T, 8], [0, 16], [1, T]]),
                   rd=(xr, xind), wr=(xr,))
            em.op("vector", "tensor_scalar", ar, ar, small[:, C_AB:C_AB + 1],
                  None, ALU.mult, rd=(ar, small), wr=(ar,))
            em.op("vector", "tensor_tensor", xr, xr, b_rep, ALU.mult,
                  rd=(xr, b_rep), wr=(xr,))
            em.op("vector", "tensor_tensor_scan", hbuf, ar, xr, 0.0,
                  ALU.mult, ALU.add, rd=(ar, xr), wr=(hbuf,))
            em.op("vector", "tensor_tensor", prod, hbuf, c_rep, ALU.mult,
                  rd=(hbuf, c_rep), wr=(prod,))
            g16 = kt % 16
            for tc8 in range(NT):
                pp = ps_y[tc8 % 2]
                em.op("tensor", "matmul", pp[:, 0:512],
                      selb[:, g16 * 128:(g16 + 1) * 128],
                      prod[:, tc8 * 512:(tc8 + 1) * 512],
                      rd=(selb, prod), wr=(pp,),
                      kw=dict(start=True, stop=True))
                ysl = yacc[:, tc8 * 512:(tc8 + 1) * 512]
                if g16 == 0:
                    em.op("scalar", "copy", ysl, pp[:, 0:512], rd=(pp,), wr=(yacc,))
                else:
                    em.op("vector", "tensor_tensor", ysl, ysl, pp[:, 0:512],
                          ALU.add, rd=(yacc, pp), wr=(yacc,))
                if g16 == 15:
                    grp = kt // 16
                    em.dma("gpsimd", yd[grp * 128:(grp + 1) * 128,
                                        tc8 * 512:(tc8 + 1) * 512],
                           ysl, rd=(yacc,), wr=(yd,))
        em.barrier()
        yk = [PF[:, k * 512:(k + 1) * 512] for k in range(KI)]
        xik = [PF[:, 8192 + k * 512:8192 + (k + 1) * 512] for k in range(KI)]
        xt4 = [PF[:, 16384 + k * 512:16384 + (k + 1) * 512] for k in range(KD)]
        ut4 = [PF[:, 20480 + k * 512:20480 + (k + 1) * 512] for k in range(KD)]
        zrot = [PF[:, 24576 + i * 512:24576 + (i + 1) * 512] for i in range(2)]
        silb = PF[:, 25600:26112]
        tca = PF[:, 26112:26624]
        yfin = [PB[:, k * 512:(k + 1) * 512] for k in range(KI)]
        opsb = [PB[:, 8192 + k * 1024:8192 + (k + 1) * 1024] for k in range(KI)]
        stgf3 = PF[:, 26624:26624 + 1024]
        for kb in range(KI):
            em.dma("sync", stgf3, op_d[kb * 128:(kb + 1) * 128, :],
                   rd=(stgf3,), wr=(stgf3,))
            em.op("vector", "tensor_copy", opsb[kb], stgf3, rd=(stgf3,), wr=(opsb[kb],))
        o1 = [PF[:, 27648 + k * 512:27648 + (k + 1) * 512] for k in range(2)]
        for c in range(NT):
            t0 = c * TC
            for kb in range(KD):
                em.dma("sync", xt4[kb], xT[kb * 128:(kb + 1) * 128, t0:t0 + TC],
                       rd=(xt4[kb],), wr=(xt4[kb],))
            emit_ln(em, xt4, ut4, None, TC, *lns, ones_col, ones_row)
            for kb in range(KI):
                em.dma("sync", yk[kb], yd[kb * 128:(kb + 1) * 128, t0:t0 + TC],
                       rd=(yk[kb], yd), wr=(yk[kb],))
                em.dma("sync", xik[kb], xind[kb * 128:(kb + 1) * 128, t0:t0 + TC],
                       rd=(xik[kb], xind), wr=(xik[kb],))
                zz = zrot[kb % 2]
                em.dma("sync", zz, zd[kb * 128:(kb + 1) * 128, t0:t0 + TC],
                       rd=(zz, zd), wr=(zz,))
                em.op("scalar", "activation", silb, zz, AF.Silu, rd=(zz,), wr=(silb,))
                em.op("vector", "scalar_tensor_tensor", tca, xik[kb],
                      small[:, C_DV + kb:C_DV + kb + 1], yk[kb], ALU.mult, ALU.add,
                      rd=(xik[kb], small, yk[kb]), wr=(tca,))
                em.op("vector", "tensor_tensor", yfin[kb], tca, silb, ALU.mult,
                      rd=(tca, silb), wr=(yfin[kb],))
            for db in range(KD):
                pp = ps_a if db % 2 == 0 else ps_b
                for kb in range(KI):
                    em.op("tensor", "matmul", pp[:, 0:TC],
                          opsb[kb][:, db * 128:(db + 1) * 128], yfin[kb],
                          rd=(opsb[kb], yfin[kb]), wr=(pp,),
                          kw=dict(start=(kb == 0), stop=(kb == KI - 1)))
                dst = o1[db % 2]
                em.op("vector", "tensor_tensor", dst, pp[:, 0:TC],
                      ut4[db], ALU.add, rd=(pp, ut4[db]), wr=(dst,))
                em.op("vector", "tensor_tensor", dst, dst,
                      xt4[db], ALU.add, rd=(dst, xt4[db]), wr=(dst,))
                em.dma("gpsimd", out_d[db * 128:(db + 1) * 128, t0:t0 + TC],
                       dst, rd=(dst,))
        em.finalize()
    return nc


_NC_CACHE = {}


def _get(name, fn):
    if name not in _NC_CACHE:
        _NC_CACHE[name] = fn()
    return _NC_CACHE[name]


def _col(v, p=128):
    return np.ascontiguousarray(np.asarray(v, np.float32).reshape(-1, p).T)


def kernel(**inp):
    cores = list(range(NCORES))
    x = np.asarray(inp["L0_attn_a_x"])
    xT = [np.ascontiguousarray(x[c].T) for c in range(NCORES)]
    identb = np.eye(128).astype(BF)
    identf = np.eye(128, dtype=np.float32)
    m1m = np.zeros((4, 128, 512), np.float32)
    mTm = np.zeros((4, 128, 512), np.float32)
    pp_ = np.arange(128)[:, None]
    ff_ = np.arange(512)[None, :]
    for o in range(4):
        m1m[o] = np.where(ff_ <= o * 128 + pp_, 0.0, -1e9)
        mTm[o] = np.where(ff_ >= pp_ + o * 128, 0.0, -1e9)
    selb = np.zeros((16, 128, 128), np.float32)
    for s in range(16):
        for p in range(128):
            selb[s, p, 8 * s + p // 16] = 1.0
    selb = selb.reshape(2048, 128).astype(BF)

    def run(name, build, im_common, per_core, outname):
        nc = _get(name, build)
        ims = []
        for c in range(NCORES):
            d = dict(im_common)
            d.update(per_core(c))
            ims.append(d)
        r = run_bass_kernel_spmd(nc, ims, core_ids=cores)
        return [r.results[c][outname] for c in range(NCORES)]

    pa_w = dict(
        Wq=inp["L0_attn_a_Wq"], Wk=inp["L0_attn_a_Wk"], Wv=inp["L0_attn_a_Wv"],
        Wg=inp["L0_attn_a_Wg"], Wo=inp["L0_attn_a_Wo"],
        bq=_col(inp["L0_attn_a_bq"]), bk=_col(inp["L0_attn_a_bk"]),
        bv=_col(inp["L0_attn_a_bv"]), bg=_col(inp["L0_attn_a_bg"]),
        bo=_col(inp["L0_attn_a_bo"]),
        ln1_g=_col(inp["L0_attn_a_ln1_g"]), ln1_b=_col(inp["L0_attn_a_ln1_b"]),
        decay_rep=_col(np.repeat(np.asarray(inp["L0_attn_a_decay_logit"]), 64)),
    )
    x2T = run("pa", lambda: build_phaseA(), pa_w, lambda c: dict(xT=xT[c]), "x2T")
    ft_w = dict(
        g=_col(inp["L0_attn_a_ln2_g"]), b=_col(inp["L0_attn_a_ln2_b"]),
        b1=_col(inp["L0_attn_a_b1"]), b2=_col(inp["L0_attn_a_b2"]),
        W1=inp["L0_attn_a_W1"], W2=inp["L0_attn_a_W2"],
    )
    s1 = run("ft", lambda: build_ffn(False, True), ft_w,
             lambda c: dict(xT=xT[c], x2T=x2T[c]), "outT")
    fq_w = dict(
        g=_col(inp["L0_ffn_ln_gamma"]), b=_col(inp["L0_ffn_ln_beta"]),
        b1=_col(inp["L0_ffn_fc1_bias"]), b2=_col(inp["L0_ffn_fc2_bias"]),
        fc1_q=inp["L0_ffn_fc1_q"], fc1_scales=inp["L0_ffn_fc1_scales"],
        fc2_q=inp["L0_ffn_fc2_q"], fc2_scales=inp["L0_ffn_fc2_scales"],
    )
    s2 = run("fq", lambda: build_ffn(True, False), fq_w,
             lambda c: dict(xT=s1[c]), "outT")
    sk_w = dict(
        Wq=inp["L0_attn_b_Wq"], Wk=inp["L0_attn_b_Wk"], Wv=inp["L0_attn_b_Wv"],
        bq=_col(inp["L0_attn_b_bq"], 64), bk=_col(inp["L0_attn_b_bk"], 64),
        bv=np.asarray(inp["L0_attn_b_bv"], np.float32).reshape(1, 512),
        Wo=inp["L0_attn_b_Wo"], bo=_col(inp["L0_attn_b_bo"]),
        identb=identb, identf=identf,
    )
    s3 = run("sk", lambda: build_sink(), sk_w, lambda c: dict(xT=s2[c]), "outT")
    s4 = run("fq", lambda: build_ffn(True, False), fq_w,
             lambda c: dict(xT=s3[c]), "outT")
    fl_w = dict(
        WqT=np.ascontiguousarray(np.asarray(inp["L1_attn_a_Wq"]).T),
        WkT=np.ascontiguousarray(np.asarray(inp["L1_attn_a_Wk"]).T),
        WvT=np.ascontiguousarray(np.asarray(inp["L1_attn_a_Wv"]).T),
        WoT=np.ascontiguousarray(np.asarray(inp["L1_attn_a_Wo"]).T),
        bq=_col(inp["L1_attn_a_bq"], 64), bk=_col(inp["L1_attn_a_bk"], 64),
        bv=np.asarray(inp["L1_attn_a_bv"], np.float32).reshape(1, 1024),
        bo=_col(inp["L1_attn_a_bo"]),
        qlsr=np.ascontiguousarray(np.asarray(inp["L1_attn_a_Wq_lsr"]).reshape(1024, 32)),
        klsr=np.ascontiguousarray(np.asarray(inp["L1_attn_a_Wk_lsr"]).reshape(1024, 32)),
        mask1=m1m.reshape(512, 512), maskT=mTm.reshape(512, 512), identf=identf,
    )
    s5 = run("fl", lambda: build_flash(), fl_w, lambda c: dict(xT=s4[c]), "outT")
    mo_w = dict(
        rw=inp["L1_ffn_router_w"],
        rb=np.asarray(inp["L1_ffn_router_b"], np.float32).reshape(8, 1),
        w1=np.ascontiguousarray(np.concatenate(
            [np.asarray(inp["L1_ffn_sw1"]).reshape(2048, 2048),
             np.asarray(inp["L1_ffn_w1"]).reshape(8192, 2048)], 0)),
        w2=np.ascontiguousarray(np.concatenate(
            [np.asarray(inp["L1_ffn_sw2"]).reshape(4096, 1024),
             np.asarray(inp["L1_ffn_w2"]).reshape(16384, 1024)], 0)),
        b1=np.concatenate([_col(np.asarray(inp["L1_ffn_sb1"])[s]) for s in range(2)] +
                          [_col(np.asarray(inp["L1_ffn_b1"])[e]) for e in range(8)], 1),
        b2=np.concatenate([_col(np.asarray(inp["L1_ffn_b2"])[e]) for e in range(8)], 1),
        sb2=np.concatenate([_col(np.asarray(inp["L1_ffn_sb2"])[s]) for s in range(2)], 1),
        identf=identf,
    )
    s6 = run("mo", lambda: build_moe(), mo_w, lambda c: dict(xT=s5[c]), "outT")
    li_w = dict(
        nw=_col(inp["L1_attn_b_norm_w"]), ipw=inp["L1_attn_b_in_proj_w"],
        apw=inp["L1_attn_b_A_proj_w"], apb=_col(inp["L1_attn_b_A_proj_b"]),
        abase=np.ascontiguousarray(
            np.tile(np.asarray(inp["L1_attn_b_A_base"], np.float32), 8).reshape(128, 1)),
        bpw=inp["L1_attn_b_B_proj_w"], cpw=inp["L1_attn_b_C_proj_w"],
        dvec=_col(inp["L1_attn_b_D"]), opw=inp["L1_attn_b_out_proj_w"], selb=selb,
    )
    s7 = run("li", lambda: build_liquid(), li_w, lambda c: dict(xT=s6[c]), "outT")
    s8 = run("mo", lambda: build_moe(), mo_w, lambda c: dict(xT=s7[c]), "outT")
    return np.stack([np.ascontiguousarray(s8[c].T) for c in range(NCORES)], 0)
